# version 03
# baseline (speedup 1.0000x reference)
import numpy as np
import ml_dtypes
import concourse.bass as bass
import concourse.mybir as mybir
from concourse.bass_utils import run_bass_kernel_spmd

F32 = mybir.dt.float32
BF16 = mybir.dt.bfloat16
F16 = mybir.dt.float16
AF = mybir.ActivationFunctionType
ALU = mybir.AluOpType
AXL = mybir.AxisListType

N_CORES = 8
ROWS = 4
T = 2048
D = 1024
NCH = D // 128
EPS_LN = 1e-5
bf16 = ml_dtypes.bfloat16

ENGINES = ("tensor", "vector", "scalar", "gpsimd", "sync")
DMA_POOL = 16



class _Op:
    __slots__ = ("idx", "eng", "fn", "reads", "writes", "is_dma", "marked",
                 "sem_val_after", "tag", "sem_key")

    def __init__(self, idx, eng, fn, reads, writes, is_dma, tag):
        self.idx, self.eng, self.fn = idx, eng, fn
        self.reads, self.writes, self.is_dma = tuple(reads), tuple(writes), is_dma
        self.marked = False
        self.sem_val_after = None
        self.tag = tag


class Prog:
    def __init__(self):
        self.ops = []
        self.last_writer = {}
        self.readers = {}
        self.deps = {}
        self.force_marked = set()

    def op(self, eng, fn, reads=(), writes=(), is_dma=False, tag="", force_mark=False):
        assert eng in ENGINES, eng
        i = len(self.ops)
        o = _Op(i, eng, fn, reads, writes, is_dma, tag)
        self.ops.append(o)
        deps = set()
        for bname in o.reads:
            w = self.last_writer.get(bname)
            if w is not None:
                deps.add(w)
        for bname in o.writes:
            w = self.last_writer.get(bname)
            if w is not None:
                deps.add(w)
            for r in self.readers.get(bname, ()):
                deps.add(r)
        deps.discard(i)
        self.deps[i] = deps
        for bname in o.reads:
            self.readers.setdefault(bname, []).append(i)
        for bname in o.writes:
            self.last_writer[bname] = i
            self.readers[bname] = []
        if force_mark:
            self.force_marked.add(i)
        return i

    def dma(self, eng, fn, reads=(), writes=(), tag="", force_mark=False):
        return self.op(eng, fn, reads, writes, is_dma=True, tag=tag,
                       force_mark=force_mark)

    def plan(self):
        eng_pos = {}
        counters = {e: 0 for e in ENGINES}
        for o in self.ops:
            eng_pos[o.idx] = counters[o.eng]
            counters[o.eng] += 1

        needs_mark = set(self.force_marked)
        for i, o in enumerate(self.ops):
            if o.is_dma:
                needs_mark.add(i)
            for d in self.deps[i]:
                do = self.ops[d]
                if do.eng == o.eng and not do.is_dma and not o.is_dma:
                    continue
                needs_mark.add(d)

        sem_val = {}
        dma_count = {e: 0 for e in ENGINES}
        for o in self.ops:
            if o.is_dma:
                slot = dma_count[o.eng] % DMA_POOL
                o.sem_key = (o.eng, slot)
                dma_count[o.eng] += 1
            else:
                o.sem_key = (o.eng,)
            if o.idx in needs_mark:
                o.marked = True
                sem_val[o.sem_key] = sem_val.get(o.sem_key, 0) + (16 if o.is_dma else 1)
                o.sem_val_after = sem_val[o.sem_key]

        self.actions = {}
        watermark = {e: {} for e in ENGINES}
        last_drain = {e: -1 for e in ENGINES}
        self.n_drains = 0
        self.n_waits = 0
        for i, o in enumerate(self.ops):
            e = o.eng
            waits = []
            drain = False
            for d in sorted(self.deps[i]):
                do = self.ops[d]
                if do.eng == e and not do.is_dma and not o.is_dma:
                    if e == "tensor":
                        continue
                    if eng_pos[d] > last_drain[e]:
                        drain = True
                else:
                    if watermark[e].get(do.sem_key, 0) < do.sem_val_after:
                        waits.append((do.sem_key, do.sem_val_after))
            if drain:
                last_drain[e] = eng_pos[i] - 1
                self.n_drains += 1
            for (sk, sv) in waits:
                watermark[e][sk] = max(watermark[e].get(sk, 0), sv)
            self.n_waits += len(waits)
            self.actions[i] = (drain, waits)

    def final_values(self):
        out = {}
        for o in self.ops:
            if o.marked:
                out[o.sem_key] = o.sem_val_after
        return out


class Builder:
    def __init__(self):
        self.nc = bass.Bass("TRN2", target_bir_lowering=False, debug=False)
        self.prog = Prog()
        self._names = set()

    def din(self, name, shape, dtype=F32):
        return self.nc.dram_tensor(name, list(shape), dtype, kind="ExternalInput").ap()

    def dout(self, name, shape, dtype=F32):
        return self.nc.dram_tensor(name, list(shape), dtype, kind="ExternalOutput").ap()

    def dtmp(self, name, shape, dtype=F32):
        return self.nc.dram_tensor(name, list(shape), dtype, kind="Internal").ap()

    def sb(self, name, shape, dtype):
        assert name not in self._names, name
        self._names.add(name)
        return self.nc.alloc_sbuf_tensor(name, list(shape), dtype)

    def ps(self, name, shape, dtype=F32):
        assert name not in self._names, name
        self._names.add(name)
        return self.nc.alloc_psum_tensor(name, list(shape), dtype)

    def finish(self):
        prog = self.prog
        prog.plan()
        nc = self.nc
        import contextlib
        with contextlib.ExitStack() as stack:
            sems = {}
            for e in ENGINES:
                sems[(e,)] = stack.enter_context(nc.semaphore(f"sem_{e}"))
            for e in ("sync", "gpsimd", "scalar", "vector", "tensor"):
                for s in range(DMA_POOL):
                    sems[(e, s)] = stack.enter_context(nc.semaphore(f"dma_{e}{s}"))
            block = stack.enter_context(nc.Block())
            finals = prog.final_values()

            nc.all_engine_barrier()
            for s in sems.values():
                nc.gpsimd.sem_clear(s)
            nc.all_engine_barrier()

            def runner(ename, eng):
                for o in prog.ops:
                    if o.eng != ename:
                        continue
                    drain, waits = prog.actions[o.idx]
                    if drain:
                        eng.drain()
                    for (sk, sv) in waits:
                        eng.wait_ge(sems[sk], sv)
                    inst = o.fn(eng)
                    if o.marked:
                        inst.then_inc(sems[o.sem_key], 16 if o.is_dma else 1)

            @block.tensor
            def _(eng):
                runner("tensor", eng)

            @block.vector
            def _(eng):
                runner("vector", eng)

            @block.scalar
            def _(eng):
                runner("scalar", eng)

            @block.gpsimd
            def _(eng):
                runner("gpsimd", eng)

            @block.sync
            def _(eng):
                runner("sync", eng)
                for sk, v in sorted(finals.items()):
                    eng.wait_ge(sems[sk], v)
        return nc


def t_slices(width=T, chunk=512):
    return [(i, min(chunk, width - i)) for i in range(0, width, chunk)]



class LnGadget:

    def __init__(self, b: Builder, width=T):
        self.b = b
        self.w = width
        self.ps_sum = b.ps("ln_ps_sum", [1, 512], F32)
        self.ps_sq = b.ps("ln_ps_sq", [1, 512], F32)
        self.ps_bc = b.ps("ln_ps_bc", [128, 1024], F32)
        self.rowA = b.sb("ln_rowA", [1, width], F32)
        self.rowB = b.sb("ln_rowB", [1, width], F32)
        self.r16 = b.sb("ln_r16", [1, width], F16)
        self.m16 = b.sb("ln_m16", [1, width], F16)
        self.onesc = b.sb("ln_onesc", [128, 1], BF16)
        self.onesr16 = b.sb("ln_onesr16", [1, 128], F16)

    def load_consts(self, ones_col, ones_row16):
        b, prog = self.b, self.b.prog
        prog.dma("sync", lambda e: e.dma_start(out=self.onesc[:], in_=ones_col[:, :]),
                 writes=["ln_onesc"])
        prog.dma("sync", lambda e: e.dma_start(out=self.onesr16[:], in_=ones_row16[:, :]),
                 writes=["ln_onesr16"])

    def stats(self, tiles, sq_tiles, d_total, bc_r, bc_r_name, bc_m, bc_m_name,
              width=None):
        b, prog = self.b, self.b.prog
        width = width or self.w
        n = len(tiles)
        for (t0, tw) in t_slices(width):
            for ki, (nm, ap) in enumerate(tiles):
                prog.op("tensor", lambda e, ap=ap, t0=t0, tw=tw, ki=ki, n=n: e.matmul(
                    self.ps_sum[0:1, 0:tw], self.onesc[:], ap[:, t0:t0 + tw],
                    start=(ki == 0), stop=(ki == n - 1)),
                    reads=["ln_onesc", nm], writes=["ln_ps_sum"])
            prog.op("vector", lambda e, t0=t0, tw=tw: e.tensor_copy(
                out=self.rowA[0:1, t0:t0 + tw], in_=self.ps_sum[0:1, 0:tw]),
                reads=["ln_ps_sum"], writes=["ln_rowA"])
            for ki, (nm, ap) in enumerate(sq_tiles):
                prog.op("tensor", lambda e, ap=ap, t0=t0, tw=tw, ki=ki, n=n: e.matmul(
                    self.ps_sq[0:1, 0:tw], self.onesc[:], ap[:, t0:t0 + tw],
                    start=(ki == 0), stop=(ki == n - 1)),
                    reads=["ln_onesc", nm], writes=["ln_ps_sq"])
            prog.op("vector", lambda e, t0=t0, tw=tw: e.tensor_copy(
                out=self.rowB[0:1, t0:t0 + tw], in_=self.ps_sq[0:1, 0:tw]),
                reads=["ln_ps_sq"], writes=["ln_rowB"])
        inv = 1.0 / d_total
        prog.op("vector", lambda e: e.tensor_scalar(
            out=self.rowA[:], in0=self.rowA[:], scalar1=inv, scalar2=None,
            op0=ALU.mult), reads=["ln_rowA"], writes=["ln_rowA"])
        prog.op("vector", lambda e: e.tensor_scalar(
            out=self.rowB[:], in0=self.rowB[:], scalar1=inv, scalar2=None,
            op0=ALU.mult), reads=["ln_rowB"], writes=["ln_rowB"])
        prog.op("vector", lambda e: e.tensor_tensor(
            out=self.r16[:], in0=self.rowA[:], in1=self.rowA[:], op=ALU.mult),
            reads=["ln_rowA"], writes=["ln_r16"])
        prog.op("vector", lambda e: e.tensor_tensor(
            out=self.rowB[:], in0=self.rowB[:], in1=self.r16[:], op=ALU.subtract),
            reads=["ln_rowB", "ln_r16"], writes=["ln_rowB"])
        prog.op("vector", lambda e: e.tensor_scalar(
            out=self.rowB[:], in0=self.rowB[:], scalar1=float(EPS_LN), scalar2=None,
            op0=ALU.add), reads=["ln_rowB"], writes=["ln_rowB"])
        prog.op("scalar", lambda e: e.activation(
            out=self.rowB[:], in_=self.rowB[:], func=AF.Ln),
            reads=["ln_rowB"], writes=["ln_rowB"])
        prog.op("scalar", lambda e: e.activation(
            out=self.rowB[:], in_=self.rowB[:], func=AF.Exp, scale=-0.5),
            reads=["ln_rowB"], writes=["ln_rowB"])
        prog.op("vector", lambda e: e.tensor_tensor(
            out=self.rowA[:], in0=self.rowA[:], in1=self.rowB[:], op=ALU.mult),
            reads=["ln_rowA", "ln_rowB"], writes=["ln_rowA"])
        prog.op("vector", lambda e: e.tensor_scalar(
            out=self.rowA[:], in0=self.rowA[:], scalar1=-1.0, scalar2=None,
            op0=ALU.mult), reads=["ln_rowA"], writes=["ln_rowA"])
        prog.op("vector", lambda e: e.tensor_copy(out=self.r16[:], in_=self.rowB[:]),
                reads=["ln_rowB"], writes=["ln_r16"])
        prog.op("vector", lambda e: e.tensor_copy(out=self.m16[:], in_=self.rowA[:]),
                reads=["ln_rowA"], writes=["ln_m16"])
        for (t0, tw) in t_slices(width):
            prog.op("tensor", lambda e, t0=t0, tw=tw: e.matmul(
                self.ps_bc[:, 0:tw], self.onesr16[:], self.r16[0:1, t0:t0 + tw],
                start=True, stop=True),
                reads=["ln_onesr16", "ln_r16"], writes=["ln_ps_bc0"])
            prog.op("tensor", lambda e, t0=t0, tw=tw: e.matmul(
                self.ps_bc[:, 512:512 + tw], self.onesr16[:], self.m16[0:1, t0:t0 + tw],
                start=True, stop=True),
                reads=["ln_onesr16", "ln_m16"], writes=["ln_ps_bc1"])
            prog.op("vector", lambda e, t0=t0, tw=tw: e.tensor_copy(
                out=bc_r[:, t0:t0 + tw], in_=self.ps_bc[:, 0:tw]),
                reads=["ln_ps_bc0"], writes=[bc_r_name])
            prog.op("vector", lambda e, t0=t0, tw=tw: e.tensor_copy(
                out=bc_m[:, t0:t0 + tw], in_=self.ps_bc[:, 512:512 + tw]),
                reads=["ln_ps_bc1"], writes=[bc_m_name])



def _pad_axis(x, n, axis):
    pad = [(0, 0)] * x.ndim
    pad[axis] = (0, n - x.shape[axis])
    return np.pad(x, pad)


def _pcols(v, pad=None):
    v = np.asarray(v)
    n = v.shape[0] if pad is None else pad
    n = ((n + 127) // 128) * 128
    v = _pad_axis(v, n, 0)
    return np.ascontiguousarray(v.reshape(-1, 128).T)


def prep_inputs(inputs):
    w = {}
    x = inputs["L0_attn_x"]
    w["xT"] = np.ascontiguousarray(x.transpose(0, 2, 1))

    w["CBT"] = np.ascontiguousarray(inputs["L1_ffn_CB"].T)
    w["SCB_p"] = _pcols(inputs["L1_ffn_SCB"])
    w["l1f_bias_p"] = _pcols(inputs["L1_ffn_bias"])

    w["inprojT"] = np.ascontiguousarray(inputs["L0_attn_in_proj_w"].T)
    w["xprojT"] = np.ascontiguousarray(inputs["L0_attn_x_proj_w"].T)
    w["dtprojT"] = np.ascontiguousarray(inputs["L0_attn_dt_proj_w"].T)
    w["outprojT"] = np.ascontiguousarray(inputs["L0_attn_out_proj_w"].T)
    cw = inputs["L0_attn_conv_w"][:, 0, :]
    w["convw_p"] = np.ascontiguousarray(cw.reshape(16, 128, 4).transpose(1, 0, 2).reshape(128, 64))
    w["convb_p"] = _pcols(inputs["L0_attn_conv_b"])
    w["dtb_p"] = _pcols(inputs["L0_attn_dt_proj_b"])
    w["Dp_p"] = _pcols(inputs["L0_attn_D_param"])
    al = inputs["L0_attn_A_log"]
    w["Alog_p"] = np.ascontiguousarray(al.reshape(16, 128, 16).transpose(1, 0, 2).reshape(128, 256))
    w["lnw_p"] = _pcols(inputs["L0_attn_ln_w"])
    w["lnb_p"] = _pcols(inputs["L0_attn_ln_b"])
    ind = np.zeros((32, 32 * 128), dtype=bf16)
    for n in range(32):
        ind[n, n * 128:(n + 1) * 128] = 1
    w["ind_all"] = ind

    half = np.arange(0, 64, 2, dtype=np.float32)
    inv_freq = 1.0 / 10000.0 ** (half / 64)
    t_ = np.arange(T, dtype=np.float32)
    freqs = np.concatenate([t_[:, None] * inv_freq[None, :]] * 2, 1)
    base_scale = (half + 0.4 * 64) / (1.4 * 64)
    power = (t_ - T // 2) / 128.0
    scale = np.concatenate([base_scale[None, :] ** power[:, None]] * 2, 1)

    def t2(a):
        aT = a.T.astype(np.float32)
        return np.ascontiguousarray(np.concatenate([aT, aT], 0)).astype(np.float16)
    w["rot_qc"] = t2(np.cos(freqs) * scale)
    w["rot_qs"] = t2(np.sin(freqs) * scale)
    w["rot_kc"] = t2(np.cos(freqs) / scale)
    w["rot_ks"] = t2(np.sin(freqs) / scale)
    R = np.zeros((128, 128), dtype=np.float32)
    for hh in range(2):
        for m in range(64):
            if m < 32:
                R[hh * 64 + m + 32, hh * 64 + m] = -1.0
            else:
                R[hh * 64 + m - 32, hh * 64 + m] = 1.0
    w["rotR"] = R.astype(bf16)
    il = np.arange(128)[:, None]
    jl = np.arange(384)[None, :]
    mgen = np.where((jl >= il) & (jl <= il + 256), 0.0, -1e9).astype(np.float32)
    m0 = mgen + np.where(jl < 256, -1e9, 0.0)
    m1 = mgen + np.where(jl < 128, -1e9, 0.0)
    w["mask_g"] = mgen.astype(bf16)
    w["mask_0"] = m0.astype(bf16)
    w["mask_1"] = m1.astype(bf16)
    w["ident"] = np.eye(128, dtype=bf16)
    w["ident64"] = np.ascontiguousarray(np.vstack([np.eye(64), np.eye(64)]).astype(bf16))
    hi = np.zeros((2, 128), dtype=np.float16)
    hi[0, 0:64] = 1.0
    hi[1, 64:128] = 1.0
    w["hind2"] = hi
    qs = inputs["L0_ffn_q_scale"]
    ks = inputs["L0_ffn_k_scale"]
    w["qs_all"] = np.ascontiguousarray(np.concatenate([qs, qs], 1).T)
    w["ks_all"] = np.ascontiguousarray(np.concatenate([ks, ks], 1).T)
    w["ffn_g"] = np.stack([_pcols(inputs["L0_ffn_ln_g"][i]) for i in range(6)])
    w["ffn_b"] = np.stack([_pcols(inputs["L0_ffn_ln_b"][i]) for i in range(6)])
    w["ffn_b2"] = np.stack([_pcols(inputs["L0_ffn_b2"][i]) for i in range(6)])
    W1 = inputs["L0_ffn_W1"]
    b1 = inputs["L0_ffn_b1"]
    W2 = inputs["L0_ffn_W2"]
    MP = 22 * 128
    W1a = np.zeros((6, D, MP), dtype=np.float32)
    W1g = np.zeros((6, D, MP), dtype=np.float32)
    b1a = np.zeros((6, MP), dtype=np.float32)
    b1g = np.zeros((6, MP), dtype=np.float32)
    W2p = np.zeros((6, MP, D), dtype=np.float32)
    W1a[:, :, :2730] = W1[:, :, :2730]
    W1g[:, :, :2730] = W1[:, :, 2730:]
    b1a[:, :2730] = b1[:, :2730]
    b1g[:, :2730] = b1[:, 2730:]
    W2p[:, :2730, :] = W2
    w["W1a"] = W1a
    w["W1g"] = W1g
    w["W2p"] = W2p
    w["b1a"] = np.stack([_pcols(b1a[i]) for i in range(6)])
    w["b1g"] = np.stack([_pcols(b1g[i]) for i in range(6)])

    w["pf_wqkv"] = np.ascontiguousarray(inputs["L1_attn_w_qkv"])
    w["pf_wout"] = np.ascontiguousarray(inputs["L1_attn_w_out"])
    w["pf_bqkv"] = _pcols(inputs["L1_attn_b_qkv"])
    w["pf_bout"] = _pcols(inputs["L1_attn_b_out"])
    w["pf_projT"] = np.ascontiguousarray(inputs["L1_attn_proj_matrix"].transpose(0, 2, 1))

    w["ones_col"] = np.ones((128, 1), dtype=bf16)
    w["ones_row16"] = np.ones((1, 128), dtype=np.float16)
    return w



def build_dequant():
    b = Builder()
    prog = b.prog
    xT = b.din("xT", [ROWS, D, T])
    CBT = b.din("CBT", [D, D])
    SCB_p = b.din("SCB_p", [128, 8])
    bias_p = b.din("l1f_bias_p", [128, 8])
    ones_col = b.din("ones_col", [128, 1], BF16)
    ones_row16 = b.din("ones_row16", [1, 128], F16)
    xT_out = b.dout("xT_out", [ROWS, D, T])

    x_f = [b.sb(f"x_f{c}", [128, T], F32) for c in range(NCH)]
    x_b = [b.sb(f"x_b{c}", [128, T], BF16) for c in range(NCH)]
    o_b = [b.sb(f"o_b{c}", [128, T], BF16) for c in range(NCH)]
    wt = [b.sb(f"wt{k}", [128, D], BF16) for k in range(NCH)]
    scb127 = b.sb("scb127", [128, 8], F32)
    bia = b.sb("bia", [128, 8], F32)
    scb_raw = b.sb("scb_raw", [128, 8], F32)
    bc_r = b.sb("bc_r", [128, T], F16)
    bc_m = b.sb("bc_m", [128, T], F16)
    ps_mm = [b.ps(f"ps_mm{i}", [128, 512], F32) for i in range(4)]
    tmp_f = b.sb("tmp_f", [128, T], F32)
    ln = LnGadget(b)

    ln.load_consts(ones_col, ones_row16)
    prog.dma("sync", lambda e: e.dma_start(out=scb_raw[:], in_=SCB_p[:, :]),
             writes=["scb_raw"])
    prog.dma("sync", lambda e: e.dma_start(out=bia[:], in_=bias_p[:, :]),
             writes=["bia"])
    prog.op("vector", lambda e: e.tensor_scalar(
        out=scb127[:], in0=scb_raw[:], scalar1=1.0 / 127.0, scalar2=None, op0=ALU.mult),
        reads=["scb_raw"], writes=["scb127"])
    for k in range(NCH):
        prog.dma("gpsimd", lambda e, k=k: e.dma_start(
            out=wt[k][:], in_=CBT[k * 128:(k + 1) * 128, :]), writes=[f"wt{k}"])

    for r in range(ROWS):
        for c in range(NCH):
            prog.dma("sync", lambda e, c=c, r=r: e.dma_start(
                out=x_f[c][:], in_=xT[r, c * 128:(c + 1) * 128, :]),
                writes=[f"x_f{c}"])
        for c in range(NCH):
            prog.op("vector", lambda e, c=c: e.tensor_copy(out=x_b[c][:], in_=x_f[c][:]),
                    reads=[f"x_f{c}"], writes=[f"x_b{c}"])
            prog.op("scalar", lambda e, c=c: e.activation(
                out=o_b[c][:], in_=x_b[c][:], func=AF.Square),
                reads=[f"x_b{c}"], writes=[f"o_b{c}"])
        ln.stats([(f"x_b{c}", x_b[c]) for c in range(NCH)],
                 [(f"o_b{c}", o_b[c]) for c in range(NCH)],
                 D, bc_r, "bc_r", bc_m, "bc_m")
        for c in range(NCH):
            prog.op("vector", lambda e, c=c: e.tensor_tensor(
                out=x_b[c][:], in0=x_b[c][:], in1=bc_r[:], op=ALU.mult),
                reads=[f"x_b{c}", "bc_r"], writes=[f"x_b{c}"])
            prog.op("vector", lambda e, c=c: e.tensor_tensor(
                out=x_b[c][:], in0=x_b[c][:], in1=bc_m[:], op=ALU.add),
                reads=[f"x_b{c}", "bc_m"], writes=[f"x_b{c}"])
        for fo in range(NCH):
            for bi, (t0, tw) in enumerate(t_slices()):
                for ki in range(NCH):
                    prog.op("tensor", lambda e, fo=fo, ki=ki, t0=t0, tw=tw, bi=bi: e.matmul(
                        ps_mm[bi][:, 0:tw], wt[ki][:, fo * 128:(fo + 1) * 128],
                        x_b[ki][:, t0:t0 + tw],
                        start=(ki == 0), stop=(ki == NCH - 1)),
                        reads=[f"wt{ki}", f"x_b{ki}"], writes=[f"ps_mm{bi}"])
                prog.op("vector", lambda e, fo=fo, t0=t0, tw=tw, bi=bi: e.tensor_scalar(
                    out=o_b[fo][:, t0:t0 + tw], in0=ps_mm[bi][:, 0:tw],
                    scalar1=scb127[:, fo:fo + 1], scalar2=bia[:, fo:fo + 1],
                    op0=ALU.mult, op1=ALU.add),
                    reads=[f"ps_mm{bi}", "scb127", "bia"], writes=[f"o_b{fo}"])
        for c in range(NCH):
            prog.op("scalar", lambda e, c=c: e.activation(
                out=x_b[c][:], in_=o_b[c][:], func=AF.Square),
                reads=[f"o_b{c}"], writes=[f"x_b{c}"])
        ln.stats([(f"o_b{c}", o_b[c]) for c in range(NCH)],
                 [(f"x_b{c}", x_b[c]) for c in range(NCH)],
                 D, bc_r, "bc_r", bc_m, "bc_m")
        for c in range(NCH):
            prog.op("vector", lambda e, c=c: e.tensor_tensor(
                out=tmp_f[:], in0=o_b[c][:], in1=bc_r[:], op=ALU.mult),
                reads=[f"o_b{c}", "bc_r"], writes=["tmp_f"])
            prog.op("vector", lambda e, c=c: e.tensor_tensor(
                out=tmp_f[:], in0=tmp_f[:], in1=bc_m[:], op=ALU.add),
                reads=["tmp_f", "bc_m"], writes=["tmp_f"])
            prog.op("vector", lambda e, c=c: e.tensor_tensor(
                out=x_f[c][:], in0=x_f[c][:], in1=tmp_f[:], op=ALU.add),
                reads=[f"x_f{c}", "tmp_f"], writes=[f"x_f{c}"])
            prog.dma("sync", lambda e, c=c, r=r: e.dma_start(
                out=xT_out[r, c * 128:(c + 1) * 128, :], in_=x_f[c][:]),
                reads=[f"x_f{c}"], force_mark=True)

    return b.finish()





HW = 1024


def build_mamba():
    b = Builder()
    prog = b.prog
    xT = b.din("xT", [ROWS, D, T])
    inprojT = b.din("inprojT", [D, 4096])
    xprojT = b.din("xprojT", [2048, 96])
    dtprojT = b.din("dtprojT", [64, 2048])
    outprojT = b.din("outprojT", [2048, D])
    convw_p = b.din("convw_p", [128, 64])
    convb_p = b.din("convb_p", [128, 16])
    dtb_p = b.din("dtb_p", [128, 16])
    Dp_p = b.din("Dp_p", [128, 16])
    Alog_p = b.din("Alog_p", [128, 256])
    lnw_p = b.din("lnw_p", [128, 8])
    lnb_p = b.din("lnb_p", [128, 8])
    ind_all = b.din("ind_all", [32, 32 * 128], BF16)
    ones_col = b.din("ones_col", [128, 1], BF16)
    ones_row16 = b.din("ones_row16", [1, 128], F16)
    xT_out = b.dout("xT_out", [ROWS, D, T])
    yT_out = b.dout("yT_out", [ROWS, D, T])

    E = 16

    f32buf = [b.sb(f"f32b{c}", [128, HW], F32) for c in range(NCH)]
    xin = [b.sb(f"xin{c}", [128, HW + 3], BF16) for c in range(E)]
    dbuf = [b.sb(f"dbuf{c}", [128, HW], BF16) for c in range(NCH)]
    yac = [b.sb(f"yac{j}", [128, HW], F32) for j in range(NCH)]
    zdram = b.dtmp("zdram", [E, 128, T], BF16)
    zsl = b.sb("zsl", [128, 512], BF16)
    zld = b.sb("zld", [128, HW], BF16)
    wbuf = [b.sb(f"wbuf{i}", [128, 2048], BF16) for i in range(2)]
    xproj_sb = b.sb("xproj_sb", [128, 16 * 96], BF16)
    dtproj_sb = b.sb("dtproj_sb", [64, 2048], BF16)
    ind_sb = b.sb("ind_sb", [32, 32 * 128], BF16)
    cw_sb = b.sb("cw_sb", [128, 64], F32)
    cb_sb = b.sb("cb_sb", [128, 16], F32)
    dtb_sb = b.sb("dtb_sb", [128, 16], F32)
    Dp_sb = b.sb("Dp_sb", [128, 16], F32)
    A_sb = b.sb("A_sb", [128, 256], F32)
    lnw_sb = b.sb("lnw_sb", [128, 8], F32)
    lnb_sb = b.sb("lnb_sb", [128, 8], F32)
    dblT = b.sb("dblT", [96, HW], F32)
    dblBC = b.sb("dblBC", [32, HW], BF16)
    dt_b = b.sb("dt_b", [64, HW], BF16)
    Bbc = b.sb("Bbc", [128, HW], BF16)
    Cbc = b.sb("Cbc", [128, HW], BF16)
    sc_a = b.sb("sc_a", [128, HW], F32)
    sc_c = b.sb("sc_c", [128, HW], F32)
    sc_bb = b.sb("sc_bb", [128, HW], BF16)
    sc_h = b.sb("sc_h", [128, HW], F16)
    sc_g = b.sb("sc_g", [128, HW], F16)
    hstate = b.sb("hstate", [128, 256], F32)
    convtail = b.sb("convtail", [128, 48], BF16)
    bc_r = b.sb("bc_r", [128, HW], F16)
    bc_m = b.sb("bc_m", [128, HW], F16)
    ps_mm = [b.ps(f"ps_mm{i}", [128, 512], F32) for i in range(4)]
    ln = LnGadget(b, width=HW)

    ln.load_consts(ones_col, ones_row16)
    for nm, dst, srcap in [("cw_sb", cw_sb, convw_p), ("cb_sb", cb_sb, convb_p),
                           ("dtb_sb", dtb_sb, dtb_p), ("Dp_sb", Dp_sb, Dp_p),
                           ("lnw_sb", lnw_sb, lnw_p), ("lnb_sb", lnb_sb, lnb_p)]:
        prog.dma("sync", lambda e, dst=dst, srcap=srcap: e.dma_start(out=dst[:], in_=srcap[:, :]),
                 writes=[nm])
    prog.dma("sync", lambda e: e.dma_start(out=A_sb[:], in_=Alog_p[:, :]), writes=["A_sb"])
    prog.op("scalar", lambda e: e.activation(out=A_sb[:], in_=A_sb[:], func=AF.Exp),
            reads=["A_sb"], writes=["A_sb"])
    prog.op("vector", lambda e: e.tensor_scalar(out=A_sb[:], in0=A_sb[:], scalar1=-1.0,
            scalar2=None, op0=ALU.mult), reads=["A_sb"], writes=["A_sb"])
    prog.dma("sync", lambda e: e.dma_start(out=ind_sb[:], in_=ind_all[:, :]), writes=["ind_sb"])
    prog.dma("gpsimd", lambda e: e.dma_start(
        out=xproj_sb.ap().rearrange("p (ki c) -> p ki c", ki=16),
        in_=xprojT.rearrange("(ki p) c -> p ki c", p=128)), writes=["xproj_sb"])
    prog.dma("gpsimd", lambda e: e.dma_start(out=dtproj_sb[:], in_=dtprojT[:, :]),
             writes=["dtproj_sb"])

    xcv = [xin[c].ap()[:, 3:3 + HW] for c in range(E)]

    for r in range(ROWS):
        for h in range(2):
            t_off = h * HW

            for c in range(NCH):
                prog.dma("sync", lambda e, c=c, r=r, t_off=t_off: e.dma_start(
                    out=f32buf[c][:], in_=xT[r, c * 128:(c + 1) * 128, t_off:t_off + HW]),
                    writes=[f"f32b{c}"])

            for c in range(NCH):
                prog.op("vector", lambda e, c=c: e.tensor_copy(out=dbuf[c][:], in_=f32buf[c][:]),
                        reads=[f"f32b{c}"], writes=[f"dbuf{c}"])
                prog.op("scalar", lambda e, c=c: e.activation(
                    out=xin[c].ap()[:, 0:HW], in_=dbuf[c][:], func=AF.Square),
                    reads=[f"dbuf{c}"], writes=[f"xin{c}"])
            ln.stats([(f"dbuf{c}", dbuf[c].ap()) for c in range(NCH)],
                     [(f"xin{c}", xin[c].ap()[:, 0:HW]) for c in range(NCH)],
                     D, bc_r, "bc_r", bc_m, "bc_m")
            for c in range(NCH):
                prog.op("vector", lambda e, c=c: e.tensor_tensor(
                    out=f32buf[c][:], in0=f32buf[c][:], in1=bc_r[:], op=ALU.mult),
                    reads=[f"f32b{c}", "bc_r"], writes=[f"f32b{c}"])
                prog.op("vector", lambda e, c=c: e.tensor_tensor(
                    out=f32buf[c][:], in0=f32buf[c][:], in1=bc_m[:], op=ALU.add),
                    reads=[f"f32b{c}", "bc_m"], writes=[f"f32b{c}"])
                prog.op("vector", lambda e, c=c: e.tensor_copy(
                    out=xin[8 + c].ap()[:, 0:HW], in_=f32buf[c][:]),
                    reads=[f"f32b{c}"], writes=[f"xin{8 + c}"])
            for c in range(NCH):
                prog.op("scalar", lambda e, c=c: e.activation(
                    out=xin[c].ap()[:, 0:HW], in_=xin[8 + c].ap()[:, 0:HW], func=AF.Square),
                    reads=[f"xin{8 + c}"], writes=[f"xin{c}"])
            ln.stats([(f"xin{8 + c}", xin[8 + c].ap()[:, 0:HW]) for c in range(NCH)],
                     [(f"xin{c}", xin[c].ap()[:, 0:HW]) for c in range(NCH)],
                     D, bc_r, "bc_r", bc_m, "bc_m")
            for c in range(NCH):
                prog.op("vector", lambda e, c=c: e.tensor_tensor(
                    out=dbuf[c][:], in0=xin[8 + c].ap()[:, 0:HW], in1=bc_r[:], op=ALU.mult),
                    reads=[f"xin{8 + c}", "bc_r"], writes=[f"dbuf{c}"])
                prog.op("vector", lambda e, c=c: e.tensor_tensor(
                    out=dbuf[c][:], in0=dbuf[c][:], in1=bc_m[:], op=ALU.add),
                    reads=[f"dbuf{c}", "bc_m"], writes=[f"dbuf{c}"])
                prog.op("vector", lambda e, c=c: e.tensor_scalar(
                    out=dbuf[c][:], in0=dbuf[c][:], scalar1=lnw_sb[:, c:c + 1],
                    scalar2=lnb_sb[:, c:c + 1], op0=ALU.mult, op1=ALU.add),
                    reads=[f"dbuf{c}", "lnw_sb", "lnb_sb"], writes=[f"dbuf{c}"])

            for dc in range(E):
                if h == 0:
                    prog.op("vector", lambda e, dc=dc: e.memset(xin[dc].ap()[:, 0:3], 0.0),
                            reads=[], writes=[f"xin{dc}"])
                else:
                    prog.op("vector", lambda e, dc=dc: e.tensor_copy(
                        out=xin[dc].ap()[:, 0:3], in_=convtail[:, dc * 3:dc * 3 + 3]),
                        reads=["convtail"], writes=[f"xin{dc}"])

            for eo in range(32):
                wb = eo % 2
                prog.dma("gpsimd", lambda e, eo=eo, wb=wb: e.dma_start(
                    out=wbuf[wb].ap()[:, 0:1024].rearrange("p (ki c) -> p ki c", ki=8),
                    in_=inprojT.rearrange("(ki p) c -> p ki c", p=128)[:, :, eo * 128:(eo + 1) * 128]),
                    writes=[f"wbuf{wb}"])
                for tq in range(2):
                    bank = (eo * 2 + tq) % 4
                    t0 = tq * 512
                    for ki in range(NCH):
                        prog.op("tensor", lambda e, eo=eo, wb=wb, ki=ki, t0=t0, bank=bank: e.matmul(
                            ps_mm[bank][:, 0:512], wbuf[wb].ap()[:, ki * 128:(ki + 1) * 128],
                            dbuf[ki][:, t0:t0 + 512],
                            start=(ki == 0), stop=(ki == NCH - 1)),
                            reads=[f"wbuf{wb}", f"dbuf{ki}"], writes=[f"ps_mm{bank}"])
                    if eo < 16:
                        prog.op("vector", lambda e, eo=eo, t0=t0, bank=bank: e.tensor_copy(
                            out=xin[eo].ap()[:, 3 + t0:3 + t0 + 512], in_=ps_mm[bank][:, 0:512]),
                            reads=[f"ps_mm{bank}"], writes=[f"xin{eo}"])
                    else:
                        prog.op("scalar", lambda e, eo=eo, t0=t0, bank=bank: e.activation(
                            out=zsl[:], in_=ps_mm[bank][:, 0:512], func=AF.Silu),
                            reads=[f"ps_mm{bank}"], writes=["zsl"])
                        prog.dma("sync", lambda e, eo=eo, t0=t0, r=r, t_off=t_off: e.dma_start(
                            out=zdram[eo - 16, :, t_off + t0:t_off + t0 + 512], in_=zsl[:]),
                            reads=["zsl"])

            for dc in range(E):
                prog.op("vector", lambda e, dc=dc: e.tensor_copy(
                    out=convtail[:, dc * 3:dc * 3 + 3], in_=xin[dc].ap()[:, HW:HW + 3]),
                    reads=[f"xin{dc}"], writes=["convtail"])

            for dc in range(E):
                prog.op("vector", lambda e, dc=dc: e.tensor_scalar(
                    out=sc_a[:], in0=xin[dc].ap()[:, 0:HW], scalar1=cw_sb[:, dc * 4:dc * 4 + 1],
                    scalar2=None, op0=ALU.mult),
                    reads=[f"xin{dc}", "cw_sb"], writes=["sc_a"])
                for j in range(1, 4):
                    prog.op("vector", lambda e, dc=dc, j=j: e.scalar_tensor_tensor(
                        out=sc_a[:], in0=xin[dc].ap()[:, j:j + HW],
                        scalar=cw_sb[:, dc * 4 + j:dc * 4 + j + 1], in1=sc_a[:],
                        op0=ALU.mult, op1=ALU.add),
                        reads=[f"xin{dc}", "cw_sb", "sc_a"], writes=["sc_a"])
                prog.op("scalar", lambda e, dc=dc: e.activation(
                    out=xin[dc].ap()[:, 3:3 + HW], in_=sc_a[:], func=AF.Silu,
                    bias=cb_sb[:, dc:dc + 1]),
                    reads=["sc_a", "cb_sb"], writes=[f"xin{dc}"])

            for tq in range(2):
                t0 = tq * 512
                bank = tq
                for ki in range(E):
                    prog.op("tensor", lambda e, ki=ki, t0=t0, bank=bank: e.matmul(
                        ps_mm[bank][0:96, 0:512], xproj_sb.ap()[:, ki * 96:(ki + 1) * 96],
                        xcv[ki][:, t0:t0 + 512],
                        start=(ki == 0), stop=(ki == E - 1)),
                        reads=["xproj_sb", f"xin{ki}"], writes=[f"ps_mm{bank}"])
                prog.op("vector", lambda e, t0=t0, bank=bank: e.tensor_copy(
                    out=dblT[:, t0:t0 + 512], in_=ps_mm[bank][0:96, 0:512]),
                    reads=[f"ps_mm{bank}"], writes=["dblT"])
            prog.op("vector", lambda e: e.tensor_copy(out=dt_b[:], in_=dblT[0:64, :]),
                    reads=["dblT"], writes=["dt_b"])
            prog.op("vector", lambda e: e.tensor_copy(out=dblBC[:], in_=dblT[64:96, :]),
                    reads=["dblT"], writes=["dblBC"])

            for g in range(2):
                for j in range(NCH):
                    dc = g * 8 + j
                    for tq in range(2):
                        t0 = tq * 512
                        bank = tq
                        prog.op("tensor", lambda e, dc=dc, t0=t0, bank=bank: e.matmul(
                            ps_mm[bank][:, 0:512], dtproj_sb.ap()[0:64, dc * 128:(dc + 1) * 128],
                            dt_b[0:64, t0:t0 + 512], start=True, stop=True),
                            reads=["dtproj_sb", "dt_b"], writes=[f"ps_mm{bank}"])
                        prog.op("vector", lambda e, dc=dc, t0=t0, bank=bank: e.tensor_scalar(
                            out=sc_c[:, t0:t0 + 512], in0=ps_mm[bank][:, 0:512],
                            scalar1=dtb_sb[:, dc:dc + 1], scalar2=30.0,
                            op0=ALU.add, op1=ALU.min),
                            reads=[f"ps_mm{bank}", "dtb_sb"], writes=["sc_c"])
                    prog.op("scalar", lambda e: e.activation(out=sc_c[:], in_=sc_c[:], func=AF.Exp),
                            reads=["sc_c"], writes=["sc_c"])
                    prog.op("vector", lambda e: e.tensor_scalar(
                        out=sc_c[:], in0=sc_c[:], scalar1=1.0, scalar2=None, op0=ALU.add),
                        reads=["sc_c"], writes=["sc_c"])
                    prog.op("scalar", lambda e, j=j: e.activation(out=dbuf[j][:], in_=sc_c[:], func=AF.Ln),
                            reads=["sc_c"], writes=[f"dbuf{j}"])
                for n in range(16):
                    for tq in range(2):
                        t0 = tq * 512
                        prog.op("tensor", lambda e, n=n, t0=t0: e.matmul(
                            ps_mm[0][:, 0:512], ind_sb.ap()[:, n * 128:(n + 1) * 128],
                            dblBC[:, t0:t0 + 512], start=True, stop=True),
                            reads=["ind_sb", "dblBC"], writes=["ps_mm0"])
                        prog.op("vector", lambda e, t0=t0: e.tensor_copy(
                            out=Bbc[:, t0:t0 + 512], in_=ps_mm[0][:, 0:512]),
                            reads=["ps_mm0"], writes=["Bbc"])
                        prog.op("tensor", lambda e, n=n, t0=t0: e.matmul(
                            ps_mm[1][:, 0:512], ind_sb.ap()[:, (16 + n) * 128:(17 + n) * 128],
                            dblBC[:, t0:t0 + 512], start=True, stop=True),
                            reads=["ind_sb", "dblBC"], writes=["ps_mm1"])
                        prog.op("vector", lambda e, t0=t0: e.tensor_copy(
                            out=Cbc[:, t0:t0 + 512], in_=ps_mm[1][:, 0:512]),
                            reads=["ps_mm1"], writes=["Cbc"])
                    for j in range(NCH):
                        dc = g * 8 + j
                        col = dc * 16 + n
                        prog.op("scalar", lambda e, j=j, col=col: e.activation(
                            out=sc_a[:], in_=dbuf[j][:], func=AF.Exp,
                            scale=A_sb[:, col:col + 1]),
                            reads=[f"dbuf{j}", "A_sb"], writes=["sc_a"])
                        prog.op("vector", lambda e, j=j, dc=dc: e.tensor_tensor(
                            out=sc_bb[:], in0=dbuf[j][:], in1=xcv[dc][:, :], op=ALU.mult),
                            reads=[f"dbuf{j}", f"xin{dc}"], writes=["sc_bb"])
                        prog.op("vector", lambda e: e.tensor_tensor(
                            out=sc_bb[:], in0=sc_bb[:], in1=Bbc[:], op=ALU.mult),
                            reads=["sc_bb", "Bbc"], writes=["sc_bb"])
                        if h == 0:
                            prog.op("vector", lambda e: e.tensor_tensor_scan(
                                out=sc_h[:], data0=sc_a[:], data1=sc_bb[:], initial=0.0,
                                op0=ALU.mult, op1=ALU.add),
                                reads=["sc_a", "sc_bb"], writes=["sc_h"])
                        else:
                            prog.op("vector", lambda e, col=col: e.tensor_tensor_scan(
                                out=sc_h[:], data0=sc_a[:], data1=sc_bb[:],
                                initial=hstate[:, col:col + 1],
                                op0=ALU.mult, op1=ALU.add),
                                reads=["sc_a", "sc_bb", "hstate"], writes=["sc_h"])
                        if h == 0:
                            prog.op("vector", lambda e, col=col: e.tensor_copy(
                                out=hstate[:, col:col + 1], in_=sc_h[:, HW - 1:HW]),
                                reads=["sc_h"], writes=["hstate"])
                        prog.op("vector", lambda e: e.tensor_tensor(
                            out=sc_g[:], in0=sc_h[:], in1=Cbc[:], op=ALU.mult),
                            reads=["sc_h", "Cbc"], writes=["sc_g"])
                        if n == 0:
                            prog.op("vector", lambda e, j=j: e.tensor_copy(
                                out=yac[j][:], in_=sc_g[:]),
                                reads=["sc_g"], writes=[f"yac{j}"])
                        else:
                            prog.op("vector", lambda e, j=j: e.tensor_tensor(
                                out=yac[j][:], in0=yac[j][:], in1=sc_g[:], op=ALU.add),
                                reads=[f"yac{j}", "sc_g"], writes=[f"yac{j}"])
                for j in range(NCH):
                    dc = g * 8 + j
                    prog.op("vector", lambda e, j=j, dc=dc: e.scalar_tensor_tensor(
                        out=yac[j][:], in0=xcv[dc][:, :], scalar=Dp_sb[:, dc:dc + 1],
                        in1=yac[j][:], op0=ALU.mult, op1=ALU.add),
                        reads=[f"xin{dc}", "Dp_sb", f"yac{j}"], writes=[f"yac{j}"])
                    prog.dma("sync", lambda e, dc=dc, t_off=t_off: e.dma_start(
                        out=zld[:], in_=zdram[dc, :, t_off:t_off + HW]),
                        reads=[], writes=["zld"])
                    prog.op("vector", lambda e, j=j, dc=dc: e.tensor_tensor(
                        out=xcv[dc][:, :], in0=yac[j][:], in1=zld[:], op=ALU.mult),
                        reads=[f"yac{j}", "zld"], writes=[f"xin{dc}"])

            for fo in range(NCH):
                wb = fo % 2
                prog.dma("gpsimd", lambda e, fo=fo, wb=wb: e.dma_start(
                    out=wbuf[wb].ap().rearrange("p (ki c) -> p ki c", ki=16),
                    in_=outprojT.rearrange("(ki p) c -> p ki c", p=128)[:, :, fo * 128:(fo + 1) * 128]),
                    writes=[f"wbuf{wb}"])
                for tq in range(2):
                    bank = (fo * 2 + tq) % 4
                    t0 = tq * 512
                    for ki in range(E):
                        prog.op("tensor", lambda e, fo=fo, wb=wb, ki=ki, t0=t0, bank=bank: e.matmul(
                            ps_mm[bank][:, 0:512], wbuf[wb].ap()[:, ki * 128:(ki + 1) * 128],
                            xcv[ki][:, t0:t0 + 512],
                            start=(ki == 0), stop=(ki == E - 1)),
                            reads=[f"wbuf{wb}", f"xin{ki}"], writes=[f"ps_mm{bank}"])
                    prog.op("vector", lambda e, fo=fo, t0=t0, bank=bank: e.tensor_tensor(
                        out=f32buf[fo][:, t0:t0 + 512], in0=ps_mm[bank][:, 0:512],
                        in1=f32buf[fo][:, t0:t0 + 512], op=ALU.add),
                        reads=[f"ps_mm{bank}", f"f32b{fo}"], writes=[f"f32b{fo}"])

            for c in range(NCH):
                prog.op("vector", lambda e, c=c: e.tensor_copy(
                    out=xin[c].ap()[:, 0:HW], in_=f32buf[c][:]),
                    reads=[f"f32b{c}"], writes=[f"xin{c}"])
                prog.op("scalar", lambda e, c=c: e.activation(
                    out=xin[8 + c].ap()[:, 0:HW], in_=xin[c].ap()[:, 0:HW], func=AF.Square),
                    reads=[f"xin{c}"], writes=[f"xin{8 + c}"])
            ln.stats([(f"xin{c}", xin[c].ap()[:, 0:HW]) for c in range(NCH)],
                     [(f"xin{8 + c}", xin[8 + c].ap()[:, 0:HW]) for c in range(NCH)],
                     D, bc_r, "bc_r", bc_m, "bc_m")
            for c in range(NCH):
                prog.dma("sync", lambda e, c=c, r=r, t_off=t_off: e.dma_start(
                    out=sc_a[:], in_=xT[r, c * 128:(c + 1) * 128, t_off:t_off + HW]),
                    writes=["sc_a"])
                prog.op("vector", lambda e, c=c: e.tensor_tensor(
                    out=sc_c[:], in0=xin[c].ap()[:, 0:HW], in1=bc_r[:], op=ALU.mult),
                    reads=[f"xin{c}", "bc_r"], writes=["sc_c"])
                prog.op("vector", lambda e, c=c: e.tensor_tensor(
                    out=sc_c[:], in0=sc_c[:], in1=bc_m[:], op=ALU.add),
                    reads=["sc_c", "bc_m"], writes=["sc_c"])
                prog.op("vector", lambda e, c=c: e.tensor_tensor(
                    out=f32buf[c][:], in0=sc_a[:], in1=sc_c[:], op=ALU.add),
                    reads=["sc_a", "sc_c"], writes=[f"f32b{c}"])
                prog.dma("sync", lambda e, c=c, r=r, t_off=t_off: e.dma_start(
                    out=xT_out[r, c * 128:(c + 1) * 128, t_off:t_off + HW], in_=f32buf[c][:]),
                    reads=[f"f32b{c}"], force_mark=True)
            for c in range(NCH):
                prog.op("vector", lambda e, c=c: e.tensor_copy(
                    out=xin[c].ap()[:, 0:HW], in_=f32buf[c][:]),
                    reads=[f"f32b{c}"], writes=[f"xin{c}"])
                prog.op("scalar", lambda e, c=c: e.activation(
                    out=xin[8 + c].ap()[:, 0:HW], in_=xin[c].ap()[:, 0:HW], func=AF.Square),
                    reads=[f"xin{c}"], writes=[f"xin{8 + c}"])
            ln.stats([(f"xin{c}", xin[c].ap()[:, 0:HW]) for c in range(NCH)],
                     [(f"xin{8 + c}", xin[8 + c].ap()[:, 0:HW]) for c in range(NCH)],
                     D, bc_r, "bc_r", bc_m, "bc_m")
            for c in range(NCH):
                prog.op("vector", lambda e, c=c: e.tensor_tensor(
                    out=sc_c[:], in0=f32buf[c][:], in1=bc_r[:], op=ALU.mult),
                    reads=[f"f32b{c}", "bc_r"], writes=["sc_c"])
                prog.op("vector", lambda e, c=c: e.tensor_tensor(
                    out=sc_c[:], in0=sc_c[:], in1=bc_m[:], op=ALU.add),
                    reads=["sc_c", "bc_m"], writes=["sc_c"])
                prog.dma("sync", lambda e, c=c, r=r, t_off=t_off: e.dma_start(
                    out=yT_out[r, c * 128:(c + 1) * 128, t_off:t_off + HW], in_=sc_c[:]),
                    reads=["sc_c"], force_mark=True)

    return b.finish()




def build_attn():
    b = Builder()
    prog = b.prog
    xT = b.din("xT", [ROWS, D, T])
    Wqkv = b.din("Wqkv", [D, 3072])
    Wo = b.din("Wo", [D, D])
    qs_c = b.din("qs_c", [128, 1])
    ks_c = b.din("ks_c", [128, 1])
    rot_qc = b.din("rot_qc", [128, T], F16)
    rot_qs = b.din("rot_qs", [128, T], F16)
    rot_kc = b.din("rot_kc", [128, T], F16)
    rot_ks = b.din("rot_ks", [128, T], F16)
    rotR = b.din("rotR", [128, 128], BF16)
    mask_g = b.din("mask_g", [128, 384], BF16)
    mask_0 = b.din("mask_0", [128, 384], BF16)
    mask_1 = b.din("mask_1", [128, 384], BF16)
    ident = b.din("ident", [128, 128], BF16)
    ident64 = b.din("ident64", [128, 64], BF16)
    hind2 = b.din("hind2", [2, 128], F16)
    ones_row16 = b.din("ones_row16", [1, 128], F16)
    xT_out = b.dout("xT_out", [ROWS, D, T])

    xres = [b.sb(f"xres{c}", [128, T], F32) for c in range(NCH)]
    xsb = [b.sb(f"xsb{c}", [128, T + 1], BF16) for c in range(NCH)]
    aout = [b.sb(f"aout{c}", [128, T], BF16) for c in range(NCH)]
    tab = {"qc": b.sb("t_qc", [128, T], F16), "qs": b.sb("t_qs", [128, T], F16),
           "kc": b.sb("t_kc", [128, T], F16), "ks": b.sb("t_ks", [128, T], F16)}
    R_sb = b.sb("R_sb", [128, 128], BF16)
    id_sb = b.sb("id_sb", [128, 128], BF16)
    id64_sb = b.sb("id64_sb", [128, 64], BF16)
    msk = [b.sb(f"msk{i}", [128, 384], BF16) for i in range(3)]
    or16 = b.sb("or16", [1, 128], F16)
    bones = b.sb("bones", [128, 2], BF16)
    hind = b.sb("hind", [2, 128], F16)
    qsc = b.sb("qsc", [128, 1], F32)
    ksc = b.sb("ksc", [128, 1], F32)
    wq = [b.sb(f"wq{i}", [128, 1024], BF16) for i in range(2)]
    wk = [b.sb(f"wk{i}", [128, 1024], BF16) for i in range(2)]
    wv = [b.sb(f"wv{i}", [128, 1024], BF16) for i in range(2)]
    qf = b.sb("qf", [128, T], BF16)
    kf = b.sb("kf", [128, 256 + T], BF16)
    vf = b.sb("vf", [128, T], BF16)
    sqs = b.sb("sqs", [128, T], BF16)
    rots = b.sb("rots", [128, T], BF16)
    nrm = b.sb("nrm", [128, T], F32)
    nr16 = b.sb("nr16", [128, T], F16)
    nbc = b.sb("nbc", [128, T], F16)
    vnat = b.sb("vnat", [128, 2048], BF16)
    simf = b.sb("simf", [128, 384], F32)
    Pb = b.sb("Pb", [128, 384], BF16)
    PT = b.sb("PT", [128, 384], BF16)
    nmax = b.sb("nmax", [128, 1], F32)
    den = b.sb("den", [128, 1], F32)
    ps_a = b.ps("ps_a", [128, 512], F32)
    ps_b = b.ps("ps_b", [128, 512], F32)
    ps_sim = b.ps("ps_sim", [128, 384], F32)
    ps_pt = b.ps("ps_pt", [128, 384], BF16)
    ps_pv = b.ps("ps_pv", [64, 128], F32)
    ps_nb = b.ps("ps_nb", [128, 512], F32)

    for nm, dst, srcap in [("t_qc", tab["qc"], rot_qc), ("t_qs", tab["qs"], rot_qs),
                           ("t_kc", tab["kc"], rot_kc), ("t_ks", tab["ks"], rot_ks),
                           ("R_sb", R_sb, rotR), ("id_sb", id_sb, ident),
                           ("msk0", msk[0], mask_g), ("msk1", msk[1], mask_0),
                           ("msk2", msk[2], mask_1), ("or16", or16, ones_row16),
                           ("id64_sb", id64_sb, ident64), ("hind", hind, hind2),
                           ("qsc", qsc, qs_c), ("ksc", ksc, ks_c)]:
        prog.dma("sync", lambda e, dst=dst, srcap=srcap: e.dma_start(out=dst[:], in_=srcap[:, :]),
                 writes=[nm])
    prog.op("vector", lambda e: e.memset(bones[:], 0.0), writes=["bones"])
    prog.op("vector", lambda e: e.memset(bones.ap()[0:64, 0:1], 1.0), writes=["bones"])
    prog.op("vector", lambda e: e.memset(bones.ap()[64:128, 1:2], 1.0), writes=["bones"])
    prog.op("vector", lambda e: e.tensor_scalar(out=qsc[:], in0=qsc[:], scalar1=8.0,
            scalar2=None, op0=ALU.mult), reads=["qsc"], writes=["qsc"])

    def rot_norm(pair, which, src_name, src_ap, dst_name, dst_ap, dst_off):
        cs = tab["qc"] if which == "q" else tab["kc"]
        sn = tab["qs"] if which == "q" else tab["ks"]
        csn, snn = ("t_qc", "t_qs") if which == "q" else ("t_kc", "t_ks")
        sc = qsc if which == "q" else ksc
        scn = "qsc" if which == "q" else "ksc"
        prog.op("scalar", lambda e: e.activation(out=sqs[:], in_=src_ap[:, :], func=AF.Square),
                reads=[src_name], writes=["sqs"])
        for t0 in range(0, T, 512):
            prog.op("tensor", lambda e, t0=t0: e.matmul(
                ps_nb[0:2, 0:512], bones[:], sqs[:, t0:t0 + 512], start=True, stop=True),
                reads=["bones", "sqs"], writes=["ps_nb"])
            prog.op("vector", lambda e, t0=t0: e.tensor_copy(
                out=nrm[0:2, t0:t0 + 512], in_=ps_nb[0:2, 0:512]),
                reads=["ps_nb"], writes=["nrm"])
        prog.op("scalar", lambda e: e.activation(out=nrm[0:2, :], in_=nrm[0:2, :], func=AF.Ln),
                reads=["nrm"], writes=["nrm"])
        prog.op("scalar", lambda e: e.activation(out=nrm[0:2, :], in_=nrm[0:2, :], func=AF.Exp,
                scale=-0.5), reads=["nrm"], writes=["nrm"])
        prog.op("vector", lambda e: e.tensor_scalar(
            out=nr16[0:2, :], in0=nrm[0:2, :], scalar1=1e12,
            scalar2=None, op0=ALU.min), reads=["nrm"], writes=["nr16"])
        for t0 in range(0, T, 512):
            prog.op("tensor", lambda e, t0=t0: e.matmul(
                ps_nb[:, 0:512], hind[:], nr16[0:2, t0:t0 + 512],
                start=True, stop=True),
                reads=["hind", "nr16"], writes=["ps_nb"])
            prog.op("vector", lambda e, t0=t0: e.tensor_copy(
                out=nbc[:, t0:t0 + 512], in_=ps_nb[:, 0:512]),
                reads=["ps_nb"], writes=["nbc"])
        prog.op("vector", lambda e: e.tensor_tensor(
            out=dst_ap[:, dst_off:dst_off + T], in0=src_ap[:, :], in1=nbc[:], op=ALU.mult),
            reads=[src_name, "nbc"], writes=[dst_name])
        prog.op("vector", lambda e: e.tensor_scalar(
            out=dst_ap[:, dst_off:dst_off + T], in0=dst_ap[:, dst_off:dst_off + T],
            scalar1=sc[:, 0:1], scalar2=None, op0=ALU.mult),
            reads=[dst_name, scn], writes=[dst_name])
        for t0 in range(0, T, 512):
            prog.op("tensor", lambda e, t0=t0: e.matmul(
                ps_nb[:, 0:512], R_sb[:], dst_ap[:, dst_off + t0:dst_off + t0 + 512],
                start=True, stop=True),
                reads=["R_sb", dst_name], writes=["ps_nb"])
            prog.op("vector", lambda e, t0=t0: e.tensor_copy(
                out=rots[:, t0:t0 + 512], in_=ps_nb[:, 0:512]),
                reads=["ps_nb"], writes=["rots"])
        prog.op("vector", lambda e: e.tensor_tensor(
            out=dst_ap[:, dst_off:dst_off + T], in0=dst_ap[:, dst_off:dst_off + T],
            in1=cs[:], op=ALU.mult), reads=[dst_name, csn], writes=[dst_name])
        prog.op("vector", lambda e: e.tensor_tensor(
            out=rots[:], in0=rots[:], in1=sn[:], op=ALU.mult),
            reads=["rots", snn], writes=["rots"])
        prog.op("vector", lambda e: e.tensor_tensor(
            out=dst_ap[:, dst_off:dst_off + T], in0=dst_ap[:, dst_off:dst_off + T],
            in1=rots[:], op=ALU.add), reads=[dst_name, "rots"], writes=[dst_name])

    for r in range(ROWS):
        for c in range(NCH):
            prog.dma("sync", lambda e, c=c, r=r: e.dma_start(
                out=xres[c][:], in_=xT[r, c * 128:(c + 1) * 128, :]), writes=[f"xres{c}"])
            prog.op("vector", lambda e, c=c: e.memset(xsb[c].ap()[:, 0:1], 0.0),
                    writes=[f"xsb{c}"])
            prog.op("vector", lambda e, c=c: e.tensor_copy(
                out=xsb[c].ap()[:, 1:T + 1], in_=xres[c][:]),
                reads=[f"xres{c}"], writes=[f"xsb{c}"])

        def shifted(c):
            if c < 4:
                return xsb[c].ap()[:, 1:T + 1]
            return xsb[c].ap()[:, 0:T]

        for pair in range(8):
            wqb, wkb, wvb = wq[pair % 2], wk[pair % 2], wv[pair % 2]
            for nm, buf, col0 in ((f"wq{pair % 2}", wqb, pair * 128),
                                  (f"wk{pair % 2}", wkb, 1024 + pair * 128),
                                  (f"wv{pair % 2}", wvb, 2048 + pair * 128)):
                prog.dma("gpsimd", lambda e, buf=buf, col0=col0: e.dma_start(
                    out=buf.ap().rearrange("p (ki c) -> p ki c", ki=8),
                    in_=Wqkv.rearrange("(ki p) c -> p ki c", p=128)[:, :, col0:col0 + 128]),
                    writes=[nm])
            for dst_name, dst_ap, off, wname, wbuf_ in (
                    ("qf", qf.ap(), 0, f"wq{pair % 2}", wqb),
                    ("kf", kf.ap(), 256, f"wk{pair % 2}", wkb),
                    ("vf", vf.ap(), 0, f"wv{pair % 2}", wvb)):
                for tq in range(4):
                    t0 = tq * 512
                    ps = ps_a if tq % 2 == 0 else ps_b
                    psn = "ps_a" if tq % 2 == 0 else "ps_b"
                    for ki in range(NCH):
                        prog.op("tensor", lambda e, wbuf_=wbuf_, ki=ki, t0=t0, ps=ps, wname=wname: e.matmul(
                            ps[:, 0:512], wbuf_.ap()[:, ki * 128:(ki + 1) * 128],
                            shifted(ki)[:, t0:t0 + 512],
                            start=(ki == 0), stop=(ki == NCH - 1)),
                            reads=[wname, f"xsb{ki}"], writes=[psn])
                    prog.op("vector", lambda e, dst_ap=dst_ap, off=off, t0=t0, ps=ps, psn=psn, dst_name=dst_name: e.tensor_copy(
                        out=dst_ap[:, off + t0:off + t0 + 512], in_=ps[:, 0:512]),
                        reads=[psn], writes=[dst_name])
            prog.op("vector", lambda e: e.memset(kf.ap()[:, 0:256], 0.0), writes=["kf"])
            rot_norm(pair, "q", "qf", qf.ap(), "qf", qf.ap(), 0)
            rot_norm(pair, "k", "kf", kf.ap()[:, 256:256 + T], "kf", kf.ap(), 256)
            for h2 in range(2):
                for ch in range(16):
                    prog.op("tensor", lambda e, h2=h2, ch=ch: e.transpose(
                        ps_pt[:, 0:64],
                        vf.ap()[h2 * 64:(h2 + 1) * 64, ch * 128:(ch + 1) * 128],
                        id64_sb[h2 * 64:(h2 + 1) * 64, 0:64]),
                        reads=["vf", "id64_sb"], writes=["ps_pt"])
                    prog.op("vector", lambda e, h2=h2, ch=ch: e.tensor_copy(
                        out=vnat[:, h2 * 1024 + ch * 64:h2 * 1024 + (ch + 1) * 64],
                        in_=ps_pt[:, 0:64]),
                        reads=["ps_pt"], writes=["vnat"])
            for h2 in range(2):
                hbase = h2 * 64
                for i in range(16):
                    mi = 1 if i == 0 else (2 if i == 1 else 0)
                    prog.op("tensor", lambda e, hbase=hbase, i=i: e.matmul(
                        ps_sim[:, 0:384], qf.ap()[hbase:hbase + 64, i * 128:(i + 1) * 128],
                        kf.ap()[hbase:hbase + 64, i * 128:i * 128 + 384],
                        start=True, stop=True),
                        reads=["qf", "kf"], writes=["ps_sim"])
                    prog.op("vector", lambda e, mi=mi: e.tensor_tensor(
                        out=simf[:], in0=ps_sim[:, 0:384], in1=msk[mi][:], op=ALU.add),
                        reads=["ps_sim", f"msk{mi}"], writes=["simf"])
                    prog.op("vector", lambda e: e.reduce_max(
                        out=nmax[:], in_=simf[:], axis=AXL.X, negate=True),
                        reads=["simf"], writes=["nmax"])
                    prog.op("scalar", lambda e: e.activation(
                        out=Pb[:], in_=simf[:], func=AF.Exp, bias=nmax[:],
                        accum_out=den[:]),
                        reads=["simf", "nmax"], writes=["Pb", "den"])
                    prog.op("vector", lambda e: e.reciprocal(out=den[:], in_=den[:]),
                            reads=["den"], writes=["den"])
                    prog.op("vector", lambda e: e.tensor_scalar(
                        out=Pb[:], in0=Pb[:], scalar1=den[:, 0:1], scalar2=None,
                        op0=ALU.mult), reads=["Pb", "den"], writes=["Pb"])
                    nj0 = 2 - i if i < 2 else 0
                    for j in range(nj0, 3):
                        prog.op("tensor", lambda e, j=j: e.transpose(
                            ps_pt[:, j * 128:(j + 1) * 128], Pb[:, j * 128:(j + 1) * 128],
                            id_sb[:]),
                            reads=["Pb", "id_sb"], writes=["ps_pt"])
                    prog.op("vector", lambda e, nj0=nj0: e.tensor_copy(
                        out=PT[:, nj0 * 128:384], in_=ps_pt[:, nj0 * 128:384]),
                        reads=["ps_pt"], writes=["PT"])
                    for j in range(nj0, 3):
                        kc_g = i - 2 + j
                        prog.op("tensor", lambda e, j=j, kc_g=kc_g, h2=h2, nj0=nj0: e.matmul(
                            ps_pv[:, 0:128],
                            vnat[:, h2 * 1024 + kc_g * 64:h2 * 1024 + (kc_g + 1) * 64],
                            PT[:, j * 128:(j + 1) * 128],
                            start=(j == nj0), stop=(j == 2)),
                            reads=["vnat", "PT"], writes=["ps_pv"])
                    prog.op("vector", lambda e, pair=pair, hbase=hbase, i=i: e.tensor_copy(
                        out=aout[pair].ap()[hbase:hbase + 64, i * 128:(i + 1) * 128],
                        in_=ps_pv[:, 0:128]),
                        reads=["ps_pv"], writes=[f"aout{pair}"])
        for fo in range(NCH):
            prog.dma("gpsimd", lambda e, fo=fo: e.dma_start(
                out=wq[fo % 2].ap().rearrange("p (ki c) -> p ki c", ki=8),
                in_=Wo.rearrange("(ki p) c -> p ki c", p=128)[:, :, fo * 128:(fo + 1) * 128]),
                writes=[f"wq{fo % 2}"])
            for tq in range(4):
                t0 = tq * 512
                ps = ps_a if tq % 2 == 0 else ps_b
                psn = "ps_a" if tq % 2 == 0 else "ps_b"
                for ki in range(NCH):
                    prog.op("tensor", lambda e, fo=fo, ki=ki, t0=t0, ps=ps: e.matmul(
                        ps[:, 0:512], wq[fo % 2].ap()[:, ki * 128:(ki + 1) * 128],
                        aout[ki].ap()[:, t0:t0 + 512],
                        start=(ki == 0), stop=(ki == NCH - 1)),
                        reads=[f"wq{fo % 2}", f"aout{ki}"], writes=[psn])
                prog.op("vector", lambda e, fo=fo, t0=t0, ps=ps, psn=psn: e.tensor_tensor(
                    out=xres[fo][:, t0:t0 + 512], in0=ps[:, 0:512],
                    in1=xres[fo][:, t0:t0 + 512], op=ALU.add),
                    reads=[psn, f"xres{fo}"], writes=[f"xres{fo}"])
            prog.dma("sync", lambda e, fo=fo, r=r: e.dma_start(
                out=xT_out[r, fo * 128:(fo + 1) * 128, :], in_=xres[fo][:]),
                reads=[f"xres{fo}"], force_mark=True)

    return b.finish()




MCH = 22


def build_ffn():
    b = Builder()
    prog = b.prog
    xT = b.din("xT", [ROWS, D, T])
    W1a = b.din("W1a", [D, MCH * 128])
    W1g = b.din("W1g", [D, MCH * 128])
    W2p = b.din("W2p", [MCH * 128, D])
    b1a_p = b.din("b1a_p", [128, MCH])
    b1g_p = b.din("b1g_p", [128, MCH])
    b2_p = b.din("b2_p", [128, 8])
    g_p = b.din("g_p", [128, 8])
    bln_p = b.din("bln_p", [128, 8])
    ones_col = b.din("ones_col", [128, 1], BF16)
    ones_row16 = b.din("ones_row16", [1, 128], F16)
    xT_out = b.dout("xT_out", [ROWS, D, T])

    xres = [b.sb(f"xres{c}", [128, T], F32) for c in range(NCH)]
    xsb = [b.sb(f"xsb{c}", [128, T + 1], BF16) for c in range(NCH)]
    sqb = [b.sb(f"sqb{c}", [128, HW], BF16) for c in range(NCH)]
    xnb = [b.sb(f"xnb{c}", [128, HW], BF16) for c in range(NCH)]
    mbuf = [b.sb(f"mbuf{m}", [128, HW], BF16) for m in range(MCH)]
    w1b = [b.sb(f"w1b{i}", [128, 1024], BF16) for i in range(4)]
    w2b = [b.sb(f"w2b{i}", [128, MCH * 128], BF16) for i in range(1)]
    b1a_sb = b.sb("b1a_sb", [128, MCH], F32)
    b1g_sb = b.sb("b1g_sb", [128, MCH], F32)
    b2_sb = b.sb("b2_sb", [128, 8], F32)
    g_sb = b.sb("g_sb", [128, 8], F32)
    bln_sb = b.sb("bln_sb", [128, 8], F32)
    gl_b = b.sb("gl_b", [128, 512], BF16)
    ab_b = b.sb("ab_b", [128, 512], BF16)
    bc_r = b.sb("bc_r", [128, HW], F16)
    bc_m = b.sb("bc_m", [128, HW], F16)
    ps_mm = [b.ps(f"ps_mm{i}", [128, 512], F32) for i in range(4)]
    ln = LnGadget(b, width=HW)

    ln.load_consts(ones_col, ones_row16)
    for nm, dst, srcap in [("b1a_sb", b1a_sb, b1a_p), ("b1g_sb", b1g_sb, b1g_p),
                           ("b2_sb", b2_sb, b2_p), ("g_sb", g_sb, g_p),
                           ("bln_sb", bln_sb, bln_p)]:
        prog.dma("sync", lambda e, dst=dst, srcap=srcap: e.dma_start(out=dst[:], in_=srcap[:, :]),
                 writes=[nm])

    for r in range(ROWS):
        for c in range(NCH):
            prog.dma("sync", lambda e, c=c, r=r: e.dma_start(
                out=xres[c][:], in_=xT[r, c * 128:(c + 1) * 128, :]), writes=[f"xres{c}"])
            prog.op("vector", lambda e, c=c: e.memset(xsb[c].ap()[:, 0:1], 0.0),
                    writes=[f"xsb{c}"])
            prog.op("vector", lambda e, c=c: e.tensor_copy(
                out=xsb[c].ap()[:, 1:T + 1], in_=xres[c][:]),
                reads=[f"xres{c}"], writes=[f"xsb{c}"])

        def shifted(c, t0, tw):
            if c < 4:
                return xsb[c].ap()[:, 1 + t0:1 + t0 + tw]
            return xsb[c].ap()[:, t0:t0 + tw]

        for h in range(2):
            t_off = h * HW
            for c in range(NCH):
                prog.op("scalar", lambda e, c=c, t_off=t_off: e.activation(
                    out=sqb[c][:], in_=shifted(c, t_off, HW), func=AF.Square),
                    reads=[f"xsb{c}"], writes=[f"sqb{c}"])
            ln.stats([(f"xsb{c}", shifted(c, t_off, HW)) for c in range(NCH)],
                     [(f"sqb{c}", sqb[c].ap()) for c in range(NCH)],
                     D, bc_r, "bc_r", bc_m, "bc_m")
            for c in range(NCH):
                prog.op("vector", lambda e, c=c, t_off=t_off: e.tensor_tensor(
                    out=xnb[c][:], in0=shifted(c, t_off, HW), in1=bc_r[:], op=ALU.mult),
                    reads=[f"xsb{c}", "bc_r"], writes=[f"xnb{c}"])
                prog.op("vector", lambda e, c=c: e.tensor_tensor(
                    out=xnb[c][:], in0=xnb[c][:], in1=bc_m[:], op=ALU.add),
                    reads=[f"xnb{c}", "bc_m"], writes=[f"xnb{c}"])
                prog.op("vector", lambda e, c=c: e.tensor_scalar(
                    out=xnb[c][:], in0=xnb[c][:], scalar1=g_sb[:, c:c + 1],
                    scalar2=bln_sb[:, c:c + 1], op0=ALU.mult, op1=ALU.add),
                    reads=[f"xnb{c}", "g_sb", "bln_sb"], writes=[f"xnb{c}"])
            for m in range(MCH):
                wa, wg = w1b[2 * (m % 2)], w1b[2 * (m % 2) + 1]
                na, ng = f"w1b{2 * (m % 2)}", f"w1b{2 * (m % 2) + 1}"
                prog.dma("gpsimd", lambda e, wa=wa, m=m: e.dma_start(
                    out=wa.ap().rearrange("p (ki c) -> p ki c", ki=8),
                    in_=W1a.rearrange("(ki p) c -> p ki c", p=128)[:, :, m * 128:(m + 1) * 128]),
                    writes=[na])
                prog.dma("gpsimd", lambda e, wg=wg, m=m: e.dma_start(
                    out=wg.ap().rearrange("p (ki c) -> p ki c", ki=8),
                    in_=W1g.rearrange("(ki p) c -> p ki c", p=128)[:, :, m * 128:(m + 1) * 128]),
                    writes=[ng])
                for tq in range(2):
                    t0 = tq * 512
                    for ki in range(NCH):
                        prog.op("tensor", lambda e, wa=wa, ki=ki, t0=t0, na=na: e.matmul(
                            ps_mm[0][:, 0:512], wa.ap()[:, ki * 128:(ki + 1) * 128],
                            xnb[ki][:, t0:t0 + 512],
                            start=(ki == 0), stop=(ki == NCH - 1)),
                            reads=[na, f"xnb{ki}"], writes=["ps_mm0"])
                    for ki in range(NCH):
                        prog.op("tensor", lambda e, wg=wg, ki=ki, t0=t0, ng=ng: e.matmul(
                            ps_mm[1][:, 0:512], wg.ap()[:, ki * 128:(ki + 1) * 128],
                            xnb[ki][:, t0:t0 + 512],
                            start=(ki == 0), stop=(ki == NCH - 1)),
                            reads=[ng, f"xnb{ki}"], writes=["ps_mm1"])
                    prog.op("vector", lambda e, m=m, t0=t0: e.tensor_scalar(
                        out=ab_b[:], in0=ps_mm[0][:, 0:512], scalar1=b1a_sb[:, m:m + 1],
                        scalar2=None, op0=ALU.add),
                        reads=["ps_mm0", "b1a_sb"], writes=["ab_b"])
                    prog.op("scalar", lambda e, m=m, t0=t0: e.activation(
                        out=gl_b[:], in_=ps_mm[1][:, 0:512], func=AF.Gelu,
                        bias=b1g_sb[:, m:m + 1]),
                        reads=["ps_mm1", "b1g_sb"], writes=["gl_b"])
                    prog.op("vector", lambda e, m=m, t0=t0: e.tensor_tensor(
                        out=mbuf[m][:, t0:t0 + 512], in0=ab_b[:], in1=gl_b[:], op=ALU.mult),
                        reads=["ab_b", "gl_b"], writes=[f"mbuf{m}"])
            for fo in range(NCH):
                wb = 0
                prog.dma("gpsimd", lambda e, fo=fo, wb=wb: e.dma_start(
                    out=w2b[wb].ap().rearrange("p (ki c) -> p ki c", ki=MCH),
                    in_=W2p.rearrange("(ki p) c -> p ki c", p=128)[:, :, fo * 128:(fo + 1) * 128]),
                    writes=[f"w2b{wb}"])
                for tq in range(2):
                    t0 = tq * 512
                    bank = 2 + tq % 2
                    for ki in range(MCH):
                        prog.op("tensor", lambda e, fo=fo, wb=wb, ki=ki, t0=t0, bank=bank: e.matmul(
                            ps_mm[bank][:, 0:512], w2b[wb].ap()[:, ki * 128:(ki + 1) * 128],
                            mbuf[ki][:, t0:t0 + 512],
                            start=(ki == 0), stop=(ki == MCH - 1)),
                            reads=[f"w2b{wb}", f"mbuf{ki}"], writes=[f"ps_mm{bank}"])
                    prog.op("vector", lambda e, fo=fo, t0=t0, bank=bank, t_off=t_off: e.scalar_tensor_tensor(
                        out=xres[fo][:, t_off + t0:t_off + t0 + 512],
                        in0=ps_mm[bank][:, 0:512], scalar=b2_sb[:, fo:fo + 1],
                        in1=xres[fo][:, t_off + t0:t_off + t0 + 512],
                        op0=ALU.add, op1=ALU.add),
                        reads=[f"ps_mm{bank}", "b2_sb", f"xres{fo}"], writes=[f"xres{fo}"])
        for fo in range(NCH):
            prog.dma("sync", lambda e, fo=fo, r=r: e.dma_start(
                out=xT_out[r, fo * 128:(fo + 1) * 128, :], in_=xres[fo][:]),
                reads=[f"xres{fo}"], force_mark=True)

    return b.finish()




PERF_DEBUG = False


def build_performer():
    b = Builder()
    prog = b.prog
    dbg = {}
    if PERF_DEBUG:
        dbg["qTf"] = b.dout("dbg_qTf", [128, T], BF16)
        dbg["qp"] = b.dout("dbg_qp", [128, 256], BF16)
        dbg["kp"] = b.dout("dbg_kp", [128, 256], BF16)
        dbg["kv"] = b.dout("dbg_kv", [128, 65], BF16)
        dbg["den"] = b.dout("dbg_den", [1, 512])
        dbg["po"] = b.dout("dbg_po", [128, T], BF16)
    xT = b.din("xT", [ROWS, D, T])
    yT = b.din("yT", [ROWS, D, T])
    wqkv = b.din("pf_wqkv", [D, 3072])
    wout = b.din("pf_wout", [D, D])
    bqkv_p = b.din("pf_bqkv", [128, 24])
    bout_p = b.din("pf_bout", [128, 8])
    projT = b.din("pf_projT", [16, 64, 256])
    ident = b.din("ident", [128, 128], BF16)
    ident64 = b.din("ident64", [128, 64], BF16)
    ones_col = b.din("ones_col", [128, 1], BF16)
    ones_row16 = b.din("ones_row16", [1, 128], F16)
    xT_out = b.dout("xT_out", [ROWS, D, T])

    DN = 64.0 ** (-0.25)
    EPSP = 1e-6

    xres = [b.sb(f"xres{c}", [128, T], F32) for c in range(NCH)]
    xn = [b.sb(f"xn{c}", [128, T], BF16) for c in range(NCH)]
    sqb = [b.sb(f"sqb{c}", [128, T], BF16) for c in range(NCH)]
    wq = [b.sb(f"wq{i}", [128, 1024], BF16) for i in range(2)]
    wk = [b.sb(f"wk{i}", [128, 1024], BF16) for i in range(2)]
    wv = [b.sb(f"wv{i}", [128, 1024], BF16) for i in range(2)]
    qTf = b.sb("qTf", [128, T], BF16)
    kTf = b.sb("kTf", [128, T], BF16)
    vTf = b.sb("vTf", [128, T], BF16)
    vnat = b.sb("vnat", [128, 2 * 16 * 65], BF16)
    projh = b.sb("projh", [128, 256], BF16)
    qpch = b.sb("qpch", [128, 256], BF16)
    kv_sb = b.sb("kv_sb", [128, 2 * 2 * 65], BF16)
    id_sb = b.sb("id_sb", [128, 128], BF16)
    id64_sb = b.sb("id64_sb", [128, 64], BF16)
    bqkv_sb = b.sb("bqkv_sb", [128, 24], F32)
    bout_sb = b.sb("bout_sb", [128, 8], F32)
    nmax = b.sb("nmax", [128, 1], F32)
    denr = b.sb("denr", [1, 512], F32)
    denr16 = b.sb("denr16", [1, 512], F16)
    rbc = b.sb("rbc", [128, 512], F16)
    bc_r = b.sb("bc_r", [128, T], F16)
    bc_m = b.sb("bc_m", [128, T], F16)
    tmp_f = b.sb("tmp_f", [128, 512], F32)
    xadd = b.sb("xadd", [128, 512], F32)
    po_sb = b.sb("po_sb", [128, T], BF16)
    poch = b.sb("poch", [64, 512], BF16)
    poT_dram = b.dtmp("poT_dram", [NCH, 128, T], BF16)
    ps_a = b.ps("ps_a", [128, 512], F32)
    ps_b = b.ps("ps_b", [128, 512], F32)
    ps_pt = b.ps("ps_pt", [128, 256], BF16)
    ln = LnGadget(b, width=T)

    ln.load_consts(ones_col, ones_row16)
    prog.dma("sync", lambda e: e.dma_start(out=bqkv_sb[:], in_=bqkv_p[:, :]), writes=["bqkv_sb"])
    prog.dma("sync", lambda e: e.dma_start(out=bout_sb[:], in_=bout_p[:, :]), writes=["bout_sb"])
    prog.dma("sync", lambda e: e.dma_start(out=id_sb[:], in_=ident[:, :]), writes=["id_sb"])
    prog.dma("sync", lambda e: e.dma_start(out=id64_sb[:], in_=ident64[:, :]), writes=["id64_sb"])
    prog.op("vector", lambda e: e.memset(
        vnat.ap().rearrange("p (g c) -> p g c", c=65)[:, :, 64:65], 1.0), writes=["vnat"])

    yld = [b.sb(f"yld{c}", [128, T], F32) for c in range(0)]

    for r in range(ROWS):
        for c in range(NCH):
            prog.dma("sync", lambda e, c=c, r=r: e.dma_start(
                out=xres[c][:], in_=yT[r, c * 128:(c + 1) * 128, :]), writes=[f"xres{c}"])
            prog.op("vector", lambda e, c=c: e.tensor_copy(out=xn[c][:], in_=xres[c][:]),
                    reads=[f"xres{c}"], writes=[f"xn{c}"])
            prog.op("scalar", lambda e, c=c: e.activation(
                out=sqb[c][:], in_=xn[c][:], func=AF.Square),
                reads=[f"xn{c}"], writes=[f"sqb{c}"])
        ln.stats([(f"xn{c}", xn[c].ap()) for c in range(NCH)],
                 [(f"sqb{c}", sqb[c].ap()) for c in range(NCH)],
                 D, bc_r, "bc_r", bc_m, "bc_m")
        for c in range(NCH):
            for t0 in range(0, T, 512):
                prog.op("vector", lambda e, c=c, t0=t0: e.tensor_tensor(
                    out=tmp_f[:], in0=xn[c][:, t0:t0 + 512], in1=bc_r[:, t0:t0 + 512],
                    op=ALU.mult), reads=[f"xn{c}", "bc_r"], writes=["tmp_f"])
                prog.op("vector", lambda e, c=c, t0=t0: e.tensor_tensor(
                    out=tmp_f[:], in0=tmp_f[:], in1=bc_m[:, t0:t0 + 512], op=ALU.add),
                    reads=["tmp_f", "bc_m"], writes=["tmp_f"])
                prog.op("vector", lambda e, c=c, t0=t0: e.tensor_copy(
                    out=xres[c][:, t0:t0 + 512], in_=tmp_f[:]),
                    reads=["tmp_f"], writes=[f"xres{c}"])
            prog.dma("sync", lambda e, c=c, r=r: e.dma_start(
                out=sqb[c][:], in_=xT[r, c * 128:(c + 1) * 128, :]),
                writes=[f"sqb{c}"]) if False else None
        for c in range(NCH):
            prog.dma("sync", lambda e, c=c, r=r: e.dma_start(
                out=tmp_big[c % 1][:] if False else xn[c][:], in_=xT[r, c * 128:(c + 1) * 128, :]),
                writes=[f"xn{c}"]) if False else None
        for c in range(NCH):
            for t0 in range(0, T, 512):
                prog.dma("sync", lambda e, c=c, r=r, t0=t0: e.dma_start(
                    out=xadd[:], in_=xT[r, c * 128:(c + 1) * 128, t0:t0 + 512]),
                    writes=["xadd"])
                prog.op("vector", lambda e, c=c, t0=t0: e.tensor_tensor(
                    out=xres[c][:, t0:t0 + 512], in0=xres[c][:, t0:t0 + 512], in1=xadd[:],
                    op=ALU.add), reads=[f"xres{c}", "xadd"], writes=[f"xres{c}"])
        for c in range(NCH):
            prog.op("vector", lambda e, c=c: e.tensor_copy(out=xn[c][:], in_=xres[c][:]),
                    reads=[f"xres{c}"], writes=[f"xn{c}"])
            prog.op("scalar", lambda e, c=c: e.activation(
                out=sqb[c][:], in_=xn[c][:], func=AF.Square),
                reads=[f"xn{c}"], writes=[f"sqb{c}"])
        ln.stats([(f"xn{c}", xn[c].ap()) for c in range(NCH)],
                 [(f"sqb{c}", sqb[c].ap()) for c in range(NCH)],
                 D, bc_r, "bc_r", bc_m, "bc_m")
        for c in range(NCH):
            prog.op("vector", lambda e, c=c: e.tensor_tensor(
                out=xn[c][:], in0=xn[c][:], in1=bc_r[:], op=ALU.mult),
                reads=[f"xn{c}", "bc_r"], writes=[f"xn{c}"])
            prog.op("vector", lambda e, c=c: e.tensor_tensor(
                out=xn[c][:], in0=xn[c][:], in1=bc_m[:], op=ALU.add),
                reads=[f"xn{c}", "bc_m"], writes=[f"xn{c}"])

        for pair in range(8):
            pb = pair % 2
            for nm, buf, col0 in ((f"wq{pb}", wq[pb], pair * 128),
                                  (f"wk{pb}", wk[pb], 1024 + pair * 128),
                                  (f"wv{pb}", wv[pb], 2048 + pair * 128)):
                prog.dma("gpsimd", lambda e, buf=buf, col0=col0: e.dma_start(
                    out=buf.ap().rearrange("p (ki c) -> p ki c", ki=8),
                    in_=wqkv.rearrange("(ki p) c -> p ki c", p=128)[:, :, col0:col0 + 128]),
                    writes=[nm])
            for dst_name, dst, wname, wbuf_, bcol, dn in (
                    ("qTf", qTf, f"wq{pb}", wq[pb], pair, True),
                    ("kTf", kTf, f"wk{pb}", wk[pb], 8 + pair, True),
                    ("vTf", vTf, f"wv{pb}", wv[pb], 16 + pair, False)):
                for tq in range(4):
                    t0 = tq * 512
                    ps = ps_a if tq % 2 == 0 else ps_b
                    psn = "ps_a" if tq % 2 == 0 else "ps_b"
                    for ki in range(NCH):
                        prog.op("tensor", lambda e, wbuf_=wbuf_, ki=ki, t0=t0, ps=ps, wname=wname: e.matmul(
                            ps[:, 0:512], wbuf_.ap()[:, ki * 128:(ki + 1) * 128],
                            xn[ki][:, t0:t0 + 512],
                            start=(ki == 0), stop=(ki == NCH - 1)),
                            reads=[wname, f"xn{ki}"], writes=[psn])
                    if dn:
                        prog.op("vector", lambda e, dst=dst, t0=t0, ps=ps, psn=psn, bcol=bcol, dst_name=dst_name: e.tensor_scalar(
                            out=dst[:, t0:t0 + 512], in0=ps[:, 0:512],
                            scalar1=bqkv_sb[:, bcol:bcol + 1], scalar2=DN,
                            op0=ALU.add, op1=ALU.mult),
                            reads=[psn, "bqkv_sb"], writes=[dst_name])
                    else:
                        prog.op("vector", lambda e, dst=dst, t0=t0, ps=ps, psn=psn, bcol=bcol, dst_name=dst_name: e.tensor_scalar(
                            out=dst[:, t0:t0 + 512], in0=ps[:, 0:512],
                            scalar1=bqkv_sb[:, bcol:bcol + 1], scalar2=None, op0=ALU.add),
                            reads=[psn, "bqkv_sb"], writes=[dst_name])
            if PERF_DEBUG and r == 0 and pair == 0:
                prog.dma("sync", lambda e: e.dma_start(out=dbg["qTf"][:, :], in_=qTf[:]), reads=["qTf"], force_mark=True)
            for h2 in range(2):
                for tc in range(16):
                    prog.op("tensor", lambda e, h2=h2, tc=tc: e.transpose(
                        ps_pt[:, 0:64],
                        vTf.ap()[h2 * 64:(h2 + 1) * 64, tc * 128:(tc + 1) * 128],
                        id64_sb[h2 * 64:h2 * 64 + 64, 0:64]),
                        reads=["vTf", "id64_sb"], writes=["ps_pt"])
                    prog.op("vector", lambda e, h2=h2, tc=tc: e.tensor_copy(
                        out=vnat[:, h2 * 1040 + tc * 65:h2 * 1040 + tc * 65 + 64],
                        in_=ps_pt[:, 0:64]),
                        reads=["ps_pt"], writes=["vnat"])
            for h2 in range(2):
                head = pair * 2 + h2
                prog.dma("gpsimd", lambda e, head=head: e.dma_start(
                    out=projh.ap()[0:64, :], in_=projT[head, :, :]), writes=["projh"])
                prog.dma("gpsimd", lambda e, head=head: e.dma_start(
                    out=projh.ap()[64:128, :], in_=projT[head, :, :]), writes=["projh"])
                for which, srcT, srcn in (("q", qTf, "qTf"), ("k", kTf, "kTf")):
                    for tc in range(16):
                        prog.op("tensor", lambda e, srcT=srcT, srcn=srcn, h2=h2, tc=tc: e.matmul(
                            ps_a[:, 0:256], srcT.ap()[h2 * 64:(h2 + 1) * 64, tc * 128:(tc + 1) * 128],
                            projh.ap()[h2 * 64:h2 * 64 + 64, :], start=True, stop=True),
                            reads=[srcn, "projh"], writes=["ps_a"])
                        prog.op("vector", lambda e: e.reduce_max(
                            out=nmax[:], in_=ps_a[:, 0:256], axis=AXL.X, negate=True),
                            reads=["ps_a"], writes=["nmax"])
                        prog.op("scalar", lambda e: e.activation(
                            out=qpch[:], in_=ps_a[:, 0:256], func=AF.Exp, bias=nmax[:]),
                            reads=["ps_a", "nmax"], writes=["qpch"])
                        prog.op("vector", lambda e: e.tensor_scalar(
                            out=qpch[:], in0=qpch[:], scalar1=1.0 / 16.0, scalar2=EPSP / 16.0,
                            op0=ALU.mult, op1=ALU.add),
                            reads=["qpch"], writes=["qpch"])
                        if PERF_DEBUG and r == 0 and pair == 0 and h2 == 1 and tc == 0:
                            nmq = "dbg_qp" if which == "q" else "dbg_kp"
                            prog.dma("sync", lambda e, nmq=nmq: e.dma_start(out=dbg[nmq[4:]][:, :], in_=qpch[:]), reads=["qpch"], force_mark=True)
                        if which == "k":
                            prog.op("vector", lambda e, h2=h2, tc=tc: e.tensor_copy(
                                out=sqb[h2 * 2 + tc // 8][:, (tc % 8) * 256:(tc % 8) * 256 + 256],
                                in_=qpch[:]),
                                reads=["qpch"], writes=[f"sqb{h2 * 2 + tc // 8}"])
                        else:
                            for mh in range(2):
                                prog.op("tensor", lambda e, mh=mh: e.transpose(
                                    ps_pt[:, 0:128], qpch[:, mh * 128:(mh + 1) * 128], id_sb[:]),
                                    reads=["qpch", "id_sb"], writes=["ps_pt"])
                                prog.op("vector", lambda e, h2=h2, mh=mh, tc=tc: e.tensor_copy(
                                    out=sqb[4 + h2 * 2 + mh][:, tc * 128:(tc + 1) * 128],
                                    in_=ps_pt[:, 0:128]),
                                    reads=["ps_pt"], writes=[f"sqb{4 + h2 * 2 + mh}"])
                for mh in range(2):
                    ps = ps_a if mh == 0 else ps_b
                    psn = "ps_a" if mh == 0 else "ps_b"
                    for tc in range(16):
                        prog.op("tensor", lambda e, h2=h2, mh=mh, tc=tc, ps=ps: e.matmul(
                            ps[:, 0:65],
                            sqb[h2 * 2 + tc // 8].ap()[:, (tc % 8) * 256 + mh * 128:(tc % 8) * 256 + mh * 128 + 128],
                            vnat[:, h2 * 1040 + tc * 65:h2 * 1040 + (tc + 1) * 65],
                            start=(tc == 0), stop=(tc == 15)),
                            reads=[f"sqb{h2 * 2 + tc // 8}", "vnat"], writes=[psn])
                    prog.op("vector", lambda e, h2=h2, mh=mh, ps=ps, psn=psn: e.tensor_copy(
                        out=kv_sb[:, (h2 * 2 + mh) * 65:(h2 * 2 + mh) * 65 + 65], in_=ps[:, 0:65]),
                        reads=[psn], writes=["kv_sb"])
                if PERF_DEBUG and r == 0 and pair == 0 and h2 == 1:
                    prog.dma("sync", lambda e: e.dma_start(out=dbg["kv"][:, :], in_=kv_sb[:, 130:195]), reads=["kv_sb"], force_mark=True)
                for tq in range(4):
                    t0 = tq * 512
                    for mh in range(2):
                        prog.op("tensor", lambda e, h2=h2, mh=mh, t0=t0: e.matmul(
                            ps_a[0:64, 0:512],
                            kv_sb[:, (h2 * 2 + mh) * 65:(h2 * 2 + mh) * 65 + 64],
                            sqb[4 + h2 * 2 + mh].ap()[:, t0:t0 + 512],
                            start=(mh == 0), stop=(mh == 1)),
                            reads=["kv_sb", f"sqb{4 + h2 * 2 + mh}"], writes=["ps_a"])
                        prog.op("tensor", lambda e, h2=h2, mh=mh, t0=t0: e.matmul(
                            ps_b[0:1, 0:512],
                            kv_sb[:, (h2 * 2 + mh) * 65 + 64:(h2 * 2 + mh) * 65 + 65],
                            sqb[4 + h2 * 2 + mh].ap()[:, t0:t0 + 512],
                            start=(mh == 0), stop=(mh == 1)),
                            reads=["kv_sb", f"sqb{4 + h2 * 2 + mh}"], writes=["ps_b"])
                    prog.op("vector", lambda e: e.tensor_scalar(
                        out=denr[:], in0=ps_b[0:1, 0:512], scalar1=EPSP, scalar2=None,
                        op0=ALU.add), reads=["ps_b"], writes=["denr"])
                    prog.op("vector", lambda e: e.reciprocal(out=denr[:], in_=denr[:]),
                            reads=["denr"], writes=["denr"])
                    if PERF_DEBUG and r == 0 and pair == 0 and h2 == 1 and t0 == 0:
                        prog.dma("sync", lambda e: e.dma_start(out=dbg["den"][:, :], in_=denr[:]), reads=["denr"], force_mark=True)
                    prog.op("vector", lambda e: e.tensor_copy(out=denr16[:], in_=denr[:]),
                            reads=["denr"], writes=["denr16"])
                    prog.op("tensor", lambda e: e.matmul(
                        ln.ps_bc[:, 0:512], ln.onesr16[:], denr16[:], start=True, stop=True),
                        reads=["ln_onesr16", "denr16"], writes=["ln_ps_bc0"])
                    prog.op("vector", lambda e: e.tensor_copy(out=rbc[:], in_=ln.ps_bc[:, 0:512]),
                            reads=["ln_ps_bc0"], writes=["rbc"])
                    prog.op("vector", lambda e: e.tensor_tensor(
                        out=poch[:], in0=ps_a[0:64, 0:512], in1=rbc[0:64, 0:512],
                        op=ALU.mult), reads=["ps_a", "rbc"], writes=["poch"])
                    prog.op("vector", lambda e, h2=h2, t0=t0: e.tensor_copy(
                        out=po_sb.ap()[h2 * 64:(h2 + 1) * 64, t0:t0 + 512], in_=poch[:]),
                        reads=["poch"], writes=["po_sb"])

            if PERF_DEBUG and r == 0 and pair == 0:
                prog.op("vector", lambda e: e.tensor_copy(out=tmp_f[:, 0:512], in_=tmp_f[:, 0:512]), reads=["tmp_f"], writes=["tmp_f"]) if False else None
                prog.dma("sync", lambda e: e.dma_start(out=dbg["po"][:, :], in_=po_sb.ap().bitcast(BF16)[:, 0:T]), reads=["po_sb"], force_mark=True) if False else None
                prog.dma("sync", lambda e: e.dma_start(out=dbg["po"][:, :], in_=po_sb[:]), reads=["po_sb"], force_mark=True)
            prog.dma("sync", lambda e, pair=pair: e.dma_start(
                out=poT_dram[pair, :, :], in_=po_sb[:]), reads=["po_sb"])
        for c in range(NCH):
            prog.dma("sync", lambda e, c=c: e.dma_start(
                out=xn[c][:], in_=poT_dram[c, :, :]), writes=[f"xn{c}"])
        for fo in range(NCH):
            pb = fo % 2
            prog.dma("gpsimd", lambda e, fo=fo, pb=pb: e.dma_start(
                out=wq[pb].ap().rearrange("p (ki c) -> p ki c", ki=8),
                in_=wout.rearrange("(ki p) c -> p ki c", p=128)[:, :, fo * 128:(fo + 1) * 128]),
                writes=[f"wq{pb}"])
            for tq in range(4):
                t0 = tq * 512
                ps = ps_a if tq % 2 == 0 else ps_b
                psn = "ps_a" if tq % 2 == 0 else "ps_b"
                for ki in range(NCH):
                    prog.op("tensor", lambda e, fo=fo, pb=pb, ki=ki, t0=t0, ps=ps: e.matmul(
                        ps[:, 0:512], wq[pb].ap()[:, ki * 128:(ki + 1) * 128],
                        xn[ki].ap()[:, t0:t0 + 512],
                        start=(ki == 0), stop=(ki == NCH - 1)),
                        reads=[f"wq{pb}", f"xn{ki}"], writes=[psn])
                prog.op("vector", lambda e, fo=fo, t0=t0, ps=ps, psn=psn: e.tensor_scalar(
                    out=sqb[fo][:, t0:t0 + 512], in0=ps[:, 0:512],
                    scalar1=bout_sb[:, fo:fo + 1], scalar2=None, op0=ALU.add),
                    reads=[psn, "bout_sb"], writes=[f"sqb{fo}"])
        for c in range(NCH):
            prog.op("scalar", lambda e, c=c: e.activation(
                out=xn[c][:], in_=sqb[c][:], func=AF.Square),
                reads=[f"sqb{c}"], writes=[f"xn{c}"])
        ln.stats([(f"sqb{c}", sqb[c].ap()) for c in range(NCH)],
                 [(f"xn{c}", xn[c].ap()) for c in range(NCH)],
                 D, bc_r, "bc_r", bc_m, "bc_m")
        for c in range(NCH):
            for t0 in range(0, T, 512):
                prog.op("vector", lambda e, c=c, t0=t0: e.tensor_tensor(
                    out=tmp_f[:], in0=sqb[c][:, t0:t0 + 512], in1=bc_r[:, t0:t0 + 512],
                    op=ALU.mult), reads=[f"sqb{c}", "bc_r"], writes=["tmp_f"])
                prog.op("vector", lambda e, c=c, t0=t0: e.tensor_tensor(
                    out=tmp_f[:], in0=tmp_f[:], in1=bc_m[:, t0:t0 + 512], op=ALU.add),
                    reads=["tmp_f", "bc_m"], writes=["tmp_f"])
                prog.op("vector", lambda e, c=c, t0=t0: e.tensor_tensor(
                    out=xres[c][:, t0:t0 + 512], in0=xres[c][:, t0:t0 + 512], in1=tmp_f[:],
                    op=ALU.add), reads=[f"xres{c}", "tmp_f"], writes=[f"xres{c}"])
            prog.dma("sync", lambda e, c=c, r=r: e.dma_start(
                out=xT_out[r, c * 128:(c + 1) * 128, :], in_=xres[c][:]),
                reads=[f"xres{c}"], force_mark=True)

    return b.finish()


_CACHE = {}


def _get(name, builder):
    if name not in _CACHE:
        _CACHE[name] = builder()
    return _CACHE[name]


def _shard(a):
    return [np.ascontiguousarray(a[c * ROWS:(c + 1) * ROWS]) for c in range(N_CORES)]


def kernel(**inputs):
    w = prep_inputs(inputs)
    cores = list(range(N_CORES))

    nc = _get("mamba", build_mamba)
    names = ["inprojT", "xprojT", "dtprojT", "outprojT", "convw_p", "convb_p",
             "dtb_p", "Dp_p", "Alog_p", "lnw_p", "lnb_p", "ind_all",
             "ones_col", "ones_row16"]
    common = {k: w[k] for k in names}
    xsh = _shard(w["xT"])
    res = run_bass_kernel_spmd(nc, [dict(common, xT=xsh[c]) for c in cores], core_ids=cores)
    x1 = [np.ascontiguousarray(r["xT_out"]) for r in res.results]
    y = [np.ascontiguousarray(r["yT_out"]) for r in res.results]

    nc_a = _get("attn", build_attn)
    nc_f = _get("ffn", build_ffn)
    for li in range(6):
        ca = {
            "Wqkv": np.ascontiguousarray(inputs["L0_ffn_Wqkv"][li]),
            "Wo": np.ascontiguousarray(inputs["L0_ffn_Wo"][li]),
            "qs_c": np.ascontiguousarray(w["qs_all"][:, li:li + 1]),
            "ks_c": np.ascontiguousarray(w["ks_all"][:, li:li + 1]),
            "rot_qc": w["rot_qc"], "rot_qs": w["rot_qs"],
            "rot_kc": w["rot_kc"], "rot_ks": w["rot_ks"],
            "rotR": w["rotR"], "mask_g": w["mask_g"], "mask_0": w["mask_0"],
            "mask_1": w["mask_1"], "ident": w["ident"], "ident64": w["ident64"],
            "hind2": w["hind2"], "ones_row16": w["ones_row16"],
        }
        res = run_bass_kernel_spmd(nc_a, [dict(ca, xT=y[c]) for c in cores], core_ids=cores)
        y = [np.ascontiguousarray(r["xT_out"]) for r in res.results]
        cf = {
            "W1a": np.ascontiguousarray(w["W1a"][li]),
            "W1g": np.ascontiguousarray(w["W1g"][li]),
            "W2p": np.ascontiguousarray(w["W2p"][li]),
            "b1a_p": np.ascontiguousarray(w["b1a"][li]),
            "b1g_p": np.ascontiguousarray(w["b1g"][li]),
            "b2_p": np.ascontiguousarray(w["ffn_b2"][li]),
            "g_p": np.ascontiguousarray(w["ffn_g"][li]),
            "bln_p": np.ascontiguousarray(w["ffn_b"][li]),
            "ones_col": w["ones_col"], "ones_row16": w["ones_row16"],
        }
        res = run_bass_kernel_spmd(nc_f, [dict(cf, xT=y[c]) for c in cores], core_ids=cores)
        y = [np.ascontiguousarray(r["xT_out"]) for r in res.results]

    nc = _get("performer", build_performer)
    names = ["pf_wqkv", "pf_wout", "pf_bqkv", "pf_bout", "pf_projT", "ident",
             "ident64", "ones_col", "ones_row16"]
    common = {k: w[k] for k in names}
    res = run_bass_kernel_spmd(nc, [dict(common, xT=x1[c], yT=y[c]) for c in cores],
                               core_ids=cores)
    x2 = [np.ascontiguousarray(r["xT_out"]) for r in res.results]

    nc = _get("dequant", build_dequant)
    common = {k: w[k] for k in ("CBT", "SCB_p", "l1f_bias_p", "ones_col", "ones_row16")}
    res = run_bass_kernel_spmd(nc, [dict(common, xT=x2[c]) for c in cores], core_ids=cores)
    out = np.concatenate([r["xT_out"] for r in res.results], axis=0)
    return np.ascontiguousarray(out.transpose(0, 2, 1))
